# version 27
# baseline (speedup 1.0000x reference)
import contextlib
import numpy as np
import concourse.bass as bass
import concourse.mybir as mybir
import bass_rust
from concourse.bass_utils import run_bass_kernel_spmd
from concourse.library_config import all_libraries as _all_libs, standard as _std_lib

F32 = mybir.dt.float32
BF16 = mybir.dt.bfloat16
AF = mybir.ActivationFunctionType
ALU = mybir.AluOpType
AX = mybir.AxisListType

D = 1024
NIC = 8
NHEAD = 16
DH = 64
NPAIR = 8
EPS = 1e-5
NCORE = 8

SEMS = ["ld", "wld", "wld0", "wld1", "wld2", "pe", "dve", "act", "pool", "st", "bar", "bar3",
        "wc", "hz", "lb"]
NTILE = 11
WQEL = 16512


class Proxy:

    def __init__(self, mk, name):
        self._mk = mk
        self._name = name

    @property
    def on(self):
        return self._mk.en == self._name

    def __getattr__(self, attr):
        mk = self._mk
        if self.on:
            real = getattr(mk.eng, attr)

            def call(*a, **k):
                ins = real(*a, **k)
                mk.last = ins
                return ins
            return call

        def noop(*a, **k):
            mk.last = None
            return None
        return noop

    def wait(self, s, n):
        if self.on:
            self._mk.eng.wait_ge(self._mk.sem[s], n)

    def inc(self, s, k=1):
        mk = self._mk
        if self.on:
            assert mk.last is not None, f"inc({s}) with no prior instruction on {self._name}"
            mk.last.then_inc(mk.sem[s], k)
        mk.cnt[s] += k

    def milestone(self, s, wait_sem=None, wait_n=None):
        if self.on:
            if wait_sem is not None:
                self._mk.eng.wait_ge(self._mk.sem[wait_sem], wait_n)
            self._mk.eng.sem_inc(self._mk.sem[s], 1)
        self._mk.cnt[s] += 1


class Marks:

    def __init__(self, mk, sem):
        self.mk = mk
        self.sem = sem
        self.lst = []

    def mark(self):
        self.lst.append(self.mk.c(self.sem))

    def wait(self, proxy, lag):
        if len(self.lst) >= lag:
            proxy.wait(self.sem, self.lst[-lag])


class MK:
    def __init__(self, nc, stack, T, input_names):
        self.nc = nc
        self.T = T
        self.NTC = T // 512
        self.NCH = T // 128
        E = stack.enter_context
        self.E = E
        self.en = None
        self.eng = None
        self.last = None

        self.SY = Proxy(self, "sync")
        self.GP = Proxy(self, "gpsimd")
        self.TE = Proxy(self, "tensor")
        self.VE = Proxy(self, "vector")
        self.SC = Proxy(self, "scalar")

        self.dram = {}
        for name, shape in input_names:
            self.dram[name] = nc.dram_tensor(name, list(shape), F32, kind="ExternalInput").ap()
        self.y_out = nc.dram_tensor("y_out", [D, T], F32, kind="ExternalOutput").ap()
        self.A_dram = nc.dram_tensor("A_scratch", [128, 8192], BF16, kind="Internal").ap()
        I16 = mybir.dt.int16
        I32 = mybir.dt.int32
        self.hsort = nc.dram_tensor("hsort_scratch", [24704, 1024], BF16, kind="Internal").ap()
        self.osort = nc.dram_tensor("osort_scratch", [24704, 1024], BF16, kind="Internal").ap()
        self.h2t_dram = nc.dram_tensor("h2t_scratch", [T, 1024], BF16, kind="Internal").ap()
        self.mg_dram = nc.dram_tensor("mg_scratch", [T, 1024], BF16, kind="Internal").ap()
        self.wab_dram = nc.dram_tensor("wab_scratch", [T, 2], F32, kind="Internal").ap()
        self.idx_dram = nc.dram_tensor("idx_scratch", [2, 16, 128], I16, kind="Internal").ap()

        self.sem = {}
        self.cnt = {}
        for s in SEMS:
            self.sem[s] = E(nc.semaphore(s))
            self.cnt[s] = 0
        self.nbar = 0
        self.nbar3 = 0
        self.agate = {}

        self.pb = [E(nc.psum_tensor(f"pb{i}", [128, 512], F32)) for i in range(6)]
        self.pt = [E(nc.psum_tensor(f"pt{i}", [128, 1024], BF16)) for i in range(2)]
        self.mid0 = contextlib.ExitStack()
        E0 = self.mid0.enter_context

        I16 = mybir.dt.int16
        I32 = mybir.dt.int32
        self.x = [E(nc.sbuf_tensor(f"x{i}", [128, T], F32)) for i in range(NIC)]
        self.arena = E(nc.sbuf_tensor("arena", [128, 8192], BF16))
        self.ones_col = E(nc.sbuf_tensor("ones_col", [128, 1], BF16))
        self.ones_row = E(nc.sbuf_tensor("ones_row", [65, 128], BF16))
        self.ident = E(nc.sbuf_tensor("ident", [128, 128], BF16))
        self.ew8 = E(nc.sbuf_tensor("ew8", [8, T], BF16))
        self.idxt = [E(nc.sbuf_tensor(f"idxt{i}", [128, 128], I16)) for i in range(2)]
        self.wqidx = E(nc.sbuf_tensor("wqidx", [128, 48], I32))
        self.ovit = [E(nc.sbuf_tensor(f"ovit{i}", [128, 64], I16)) for i in range(3)]
        self.ebase = E0(nc.sbuf_tensor("ebase", [8, 2], F32))
        self.ovc = E0(nc.sbuf_tensor("ovc", [128, 64], F32))
        self.maskUT = E0(nc.sbuf_tensor("maskUT", [128, 128], BF16))
        self.hpat = E0(nc.sbuf_tensor("hpat", [128, 2], BF16))
        self.pat_plain = E0(nc.sbuf_tensor("pat_plain", [2, 128], BF16))
        self.pat_l1a = E0(nc.sbuf_tensor("pat_l1a", [2, D], BF16))
        self.pat_ln1 = E0(nc.sbuf_tensor("pat_ln1", [2, D], BF16))
        self.pat_ln2 = E0(nc.sbuf_tensor("pat_ln2", [2, D], BF16))
        self.hl_q = E0(nc.sbuf_tensor("hl_q", [3, 128], BF16))
        self.hl_k = E0(nc.sbuf_tensor("hl_k", [3, 128], BF16))
        self.rsm = E0(nc.sbuf_tensor("rsm", [2, 2048], BF16))
        self.mr3 = E0(nc.sbuf_tensor("mr3", [3, 1024], BF16))
        self.bias = E0(nc.sbuf_tensor("sb_bias", [128, 240], F32))
        self.gateb = E0(nc.sbuf_tensor("sb_gateb", [8, 1], F32))
        self.sq0 = E0(nc.sbuf_tensor("sq0", [128, 512], BF16))
        self.sq1 = E0(nc.sbuf_tensor("sq1", [128, 512], BF16))
        self.ccol = E0(nc.sbuf_tensor("ccol", [128, 8], F32))
        self.gatew = E0(nc.sbuf_tensor("gatew", [128, 64], BF16))
        self.c1b = E0(nc.sbuf_tensor("c1b", [1, 8], BF16))
        self.irow = E0(nc.sbuf_tensor("irow", [8, 16], F32))
        self.pgt = E0(nc.sbuf_tensor("pgt", [128, 4], F32))

        self.bslot = {"l0ffn": 0, "fwd_bz": 8, "fwd_bh": 40, "bwd_bz": 72,
                      "bwd_bh": 104, "fus": 136, "bqkv": 144, "bo": 168, "b2": 176}

    def aslot(self, off, ln):
        assert off + ln <= 8192
        return self.arena[:, off:off + ln]

    def wload(self, key, out_ap, src_ap, ws):
        g = None
        if isinstance(key, tuple) and isinstance(key[0], int):
            o, ln_ = key
            for kk, v in list(self.agate.items()):
                if (isinstance(kk, tuple) and len(kk) == 2 and isinstance(kk[0], int)
                        and kk[0] < o + ln_ and o < kk[0] + kk[1]):
                    g = v if g is None else max(g, v)
        else:
            g = self.agate.get(key)
        if g is not None:
            self.GP.wait("pe", g)
        self.GP.dma_start(out=out_ap, in_=src_ap)
        self.GP.inc(ws, 16)

    def wdone(self, *keys):
        for k in keys:
            if isinstance(k, tuple) and isinstance(k[0], int):
                o, ln_ = k
                for (ko, kl) in [kk for kk in self.agate
                                 if isinstance(kk, tuple) and isinstance(kk[0], int)]:
                    if ko >= o and ko + kl <= o + ln_:
                        del self.agate[(ko, kl)]
            self.agate[k] = self.c("pe")

    def marks(self, sem):
        return Marks(self, sem)

    def bar(self, full=False):
        if full:
            self.nbar += 1
            self.eng.sem_inc(self.sem["bar"], 1)
            self.eng.wait_ge(self.sem["bar"], 5 * self.nbar)
        else:
            self.nbar3 += 1
            if self.en in ("tensor", "vector", "scalar"):
                self.eng.sem_inc(self.sem["bar3"], 1)
                self.eng.wait_ge(self.sem["bar3"], 3 * self.nbar3)

    def c(self, s):
        return self.cnt[s]

    def salloc(self, name, shape, dtype):
        if name not in self._scoped:
            assert self._first_pass, f"salloc({name}) introduced after first pass"
            self._scoped[name] = self._phase_stack.enter_context(
                self.nc.sbuf_tensor(name, list(shape), dtype))
        return self._scoped[name]

    def run_phase(self, fn):
        cnt0 = dict(self.cnt)
        nbar0 = self.nbar
        nbar30 = self.nbar3
        agate0 = dict(self.agate)
        final = None
        self._phase_stack = contextlib.ExitStack()
        self._scoped = {}
        regs = (("sync", self.block.sync), ("gpsimd", self.block.gpsimd),
                ("tensor", self.block.tensor), ("vector", self.block.vector),
                ("scalar", self.block.scalar))
        for en, reg in regs:
            self.cnt = dict(cnt0)
            self.nbar = nbar0
            self.nbar3 = nbar30
            self.agate = dict(agate0)
            self.en = en
            self._first_pass = final is None

            def _run(e, _fn=fn):
                self.eng = e
                _fn()
            reg(_run)
            if final is None:
                final = (dict(self.cnt), self.nbar, dict(self.agate), self.nbar3)
            else:
                assert (self.cnt, self.nbar, self.agate, self.nbar3) == \
                    (final[0], final[1], final[2], final[3]), \
                    f"{en}: counter divergence {self.cnt} vs {final[0]}"
        self._phase_stack.close()
        self.cnt = dict(final[0])
        self.nbar = final[1]
        self.agate = dict(final[2])
        self.nbar3 = final[3]

    def ph_consts(self):
        SY, GP = self.SY, self.GP
        T = self.T
        for ic in range(NIC):
            SY.dma_start(out=self.x[ic][:], in_=self.dram["xT"][ic * 128:(ic + 1) * 128, :])
            SY.inc("ld", 16)
        SY.dma_start(out=self.bias[:], in_=self.dram["biascols"])
        SY.inc("ld", 16)
        SY.dma_start(out=self.gateb[:], in_=self.dram["gate_b"])
        SY.inc("ld", 16)
        for dst, src in [
            (self.ones_col[:], self.dram["c_onescol"]),
            (self.ones_row[:], self.dram["c_ones128"][0:65, :]),
            (self.ident[:], self.dram["c_ident"]),
            (self.maskUT[:], self.dram["c_maskUT"]),
            (self.hpat[:], self.dram["c_hpat"]),
            (self.pat_plain[:], self.dram["c_patplain"]),
            (self.pat_l1a[:], self.dram["g_l1a"]),
            (self.pat_ln1[:], self.dram["g_ln1"]),
            (self.pat_ln2[:], self.dram["g_ln2"]),
            (self.hl_q[:], self.dram["g_hlq"]),
            (self.hl_k[:], self.dram["g_hlk"]),
            (self.rsm[1:2, :], self.dram["c_zo"]),
            (self.mr3[2:3, :], self.dram["c_onesT"][0:1, 0:1024]),
            (self.irow[0:8, 0:11], self.dram["c_irow"]),
            (self.pgt[:], self.dram["c_pg"]),
            (self.ebase[:], self.dram["c_ebase"]),
            (self.ovc[:], self.dram["c_ovc"]),
        ]:
            GP.dma_start(out=dst, in_=src)
            GP.inc("wld", 16)
        SY.wait("ld", self.c("ld"))
        GP.wait("wld", self.c("wld"))
        self.bar(full=True)


    def ln(self, src, out, pat, save=False, src_f32=True, fillers=None):
        T, NTC = self.T, self.NTC
        SY, GP, TE, VE, SC = self.SY, self.GP, self.TE, self.VE, self.SC
        c1 = 1.0 / D
        sb = out if src_f32 else src
        if src_f32:
            for ic in range(NIC):
                SC.wait("dve", self.c("dve"))
                SC.copy(out[ic][:], src[ic][:])
                SC.inc("act")
        entry_dve = self.c("dve")
        cast_done = self.c("act")
        pe_stats = {}
        rows_read = {}
        rows_done = {}
        lnexp_done = {}
        bc_done = {}
        lm = self.marks("dve")

        def emit_stats(tc):
            ts = slice(tc * 512, (tc + 1) * 512)
            pe_base = self.c("pe")
            for ic in range(NIC):
                sq = self.sq0 if ic % 2 == 0 else self.sq1
                SC.wait("act", cast_done)
                if ic == 0 and tc == 0:
                    SC.wait("dve", entry_dve)
                if ic >= 2:
                    SC.wait("pe", pe_base + ic - 1)
                elif tc > 0:
                    SC.wait("pe", pe_base)
                SC.activation(sq[:, 0:512], sb[ic][:, ts], AF.Square)
                SC.inc("act")
                TE.wait("act", self.c("act"))
                if ic == 0 and tc > 1:
                    TE.wait("dve", rows_read[tc - 2])
                if ic == 0 and tc <= 1:
                    TE.wait("dve", entry_dve)
                sbank = self.pb[4 + tc % 2]
                TE.matmul(sbank[0:1, 0:512], self.ones_col[:], sb[ic][:, ts],
                          start=(ic == 0), stop=(ic == NIC - 1))
                TE.matmul(sbank[32:33, 0:512], self.ones_col[:], sq[:, 0:512],
                          start=(ic == 0), stop=(ic == NIC - 1))
                TE.inc("pe")
            pe_stats[tc] = self.c("pe")

        def emit_rows(tc):
            par = (tc % 2) * 512
            ps = slice(par, par + 512)
            VE.wait("pe", pe_stats[tc])
            if tc >= 2:
                VE.wait("act", lnexp_done[tc - 2])
                VE.wait("pe", bc_done.get(tc - 2, 0))
            sbank = self.pb[4 + tc % 2]
            VE.tensor_scalar_mul(self.rw0[0:1, ps], sbank[0:1, 0:512], c1)
            VE.tensor_scalar_mul(self.rw1[0:1, ps], sbank[32:33, 0:512], c1)
            VE.inc("dve")
            rows_read[tc] = self.c("dve")
            VE.wait("dve", self.c("dve"))
            VE.tensor_mul(self.rw2[0:1, ps], self.rw0[0:1, ps], self.rw0[0:1, ps])
            VE.inc("dve")
            VE.wait("dve", self.c("dve"))
            VE.scalar_tensor_tensor(self.rw1[0:1, ps], self.rw1[0:1, ps], EPS,
                                    self.rw2[0:1, ps], ALU.add, ALU.subtract)
            VE.inc("dve")
            SC.wait("dve", self.c("dve"))
            SC.activation(self.rw1[0:1, ps], self.rw1[0:1, ps], AF.Ln)
            SC.inc("act")
            SC.wait("act", self.c("act"))
            SC.activation(self.rw2[0:1, ps], self.rw1[0:1, ps], AF.Exp, scale=-0.5)
            SC.inc("act")
            lnexp_done[tc] = self.c("act")
            VE.wait("act", self.c("act"))
            VE.scalar_tensor_tensor(self.rsm[0:1, (tc % 2) * 1024 + 512:(tc % 2) * 1024 + 1024],
                                    self.rw0[0:1, ps], -1.0,
                                    self.rw2[0:1, ps], ALU.mult, ALU.mult)
            VE.inc("dve")
            VE.tensor_copy(self.rsm[0:1, (tc % 2) * 1024:(tc % 2) * 1024 + 512],
                           self.rw2[0:1, ps])
            VE.inc("dve")
            rows_done[tc] = self.c("dve")

        def emit_apply(tc):
            ts = slice(tc * 512, (tc + 1) * 512)
            rbase = (tc % 2) * 1024
            for ic in range(NIC):
                bi = ic % 2
                if ic < 2:
                    TE.wait("dve", rows_done[tc])
                lm.wait(TE, 2)
                po = (ic * 128) % pat.shape[1]
                TE.matmul(self.pb[bi][:, 0:512], pat[0:1, po:po + 128],
                          self.rsm[0:1, rbase:rbase + 512], start=True, stop=True)
                TE.matmul(self.pb[2 + bi][:, 0:512], pat[0:2, po:po + 128],
                          self.rsm[0:2, rbase + 512:rbase + 1024], start=True, stop=True)
                TE.inc("pe")
                VE.wait("pe", self.c("pe"))
                VE.tensor_tensor(out[ic][:, ts], src[ic][:, ts], self.pb[bi][:, 0:512], ALU.mult)
                VE.inc("dve")
                VE.wait("dve", self.c("dve"))
                VE.tensor_tensor(out[ic][:, ts], out[ic][:, ts], self.pb[2 + bi][:, 0:512], ALU.add)
                VE.inc("dve")
                lm.mark()
            bc_done[tc] = self.c("pe")

        emit_stats(0)
        for tc in range(NTC):
            if tc + 1 < NTC:
                emit_stats(tc + 1)
            emit_rows(tc)
            if tc >= 1:
                emit_apply(tc - 1)
                if fillers:
                    fillers(tc - 1)
        emit_apply(NTC - 1)
        if fillers:
            fillers(NTC - 1)

    def proj(self, gname, rhs_tiles, nk, evac):
        T, NTC = self.T, self.NTC
        GP, TE = self.GP, self.TE
        mkv = self.marks("dve")
        for gr in range(2):
            w_ = self.aslot(4096 * (gr % 2), 4096)
            ws = "wld0" if gr % 2 == 0 else "wld1"
            self.wload((4096 * (gr % 2), 4096), w_, self.dram[gname][gr], ws)
            wld_here = (ws, self.c(ws))
            for oc4 in range(4):
                oc = gr * 4 + oc4
                for tc in range(NTC):
                    ts = slice(tc * 512, (tc + 1) * 512)
                    bank = (oc * NTC + tc) % 4
                    if oc4 == 0 and tc == 0:
                        TE.wait(*wld_here)
                        if gr == 0:
                            TE.wait("dve", self.c("dve"))
                    mkv.wait(TE, 4)
                    for ic in range(nk):
                        TE.matmul(self.pb[bank][:, 0:512],
                                  w_[:, oc4 * 1024 + ic * 128:oc4 * 1024 + (ic + 1) * 128],
                                  rhs_tiles[ic][:, ts], start=(ic == 0), stop=(ic == nk - 1))
                    TE.inc("pe")
                    evac(oc, tc, ts, bank)
                    mkv.mark()
            self.wdone((4096 * (gr % 2), 4096))

    def ph_l0_attn(self):
        T, NTC, NCH = self.T, self.NTC, self.NCH
        SY, GP, TE, VE, SC = self.SY, self.GP, self.TE, self.VE, self.SC
        self.ktr = self.salloc("ktr", [128, T], BF16)
        self.vtr = self.salloc("vtr", [128, T], BF16)
        self.spref = self.salloc("spref", [128, 16 * DH], BF16)
        self.ln(self.x, self.sA, self.pat_plain, src_f32=True)
        for pair in range(NPAIR):
            pb_ = pair % 2
            blk = self.aslot(3072 * pb_, 3072)
            wv, wq, wk = (self.aslot(3072 * pb_, 1024), self.aslot(3072 * pb_ + 1024, 1024),
                          self.aslot(3072 * pb_ + 2048, 1024))
            qt, kt, vt = self.sC[0], self.sC[1], self.sC[2]
            ws = "wld0" if pair % 2 == 0 else "wld1"
            if pair >= 1:
                GP.wait("dve", self.c("dve"))
            self.wload((3072 * pb_, 3072), blk, self.dram["l0_wqkv"][pair], ws)
            wld_here = (ws, self.c(ws))
            pm = self.marks("dve")
            for which, (wt, dst) in enumerate([(wv, vt), (wq, qt), (wk, kt)]):
                for tc in range(NTC):
                    ts = slice(tc * 512, (tc + 1) * 512)
                    bank = (which * NTC + tc) % 4
                    if which == 0 and tc == 0:
                        TE.wait(*wld_here)
                        TE.wait("dve", self.c("dve"))
                    pm.wait(TE, 4)
                    for ic in range(NIC):
                        TE.matmul(self.pb[bank][:, 0:512], wt[:, ic * 128:(ic + 1) * 128],
                                  self.sA[ic][:, ts], start=(ic == 0), stop=(ic == NIC - 1))
                    TE.inc("pe")
                    VE.wait("pe", self.c("pe"))
                    VE.tensor_copy(dst[:, ts], self.pb[bank][:, 0:512])
                    VE.inc("dve")
                    pm.mark()
            self.wdone((3072 * pb_, 3072))
            vfill_marks = self.marks("dve")
            vfill_entry = self.c("dve")

            def mk_vfill(ch, _vt=vt):
                def go():
                    TE, VE = self.TE, self.VE
                    cs = slice(ch * 128, (ch + 1) * 128)
                    tb = ch % 2
                    if ch < 2:
                        TE.wait("dve", vfill_entry)
                    vfill_marks.wait(TE, 2)
                    TE.transpose(self.pt[tb][:, 128:256], _vt[:, cs], self.ident[:])
                    TE.inc("pe")
                    VE.wait("pe", self.c("pe"))
                    VE.tensor_copy(self.vtr[:, cs], self.pt[tb][:, 128:256])
                    VE.inc("dve")
                    vfill_marks.mark()
                return go
            self.head_ln2(qt, kt, [mk_vfill(ch) for ch in range(self.NCH)])
            self.l0_linattn(qt, kt, vt, pair)
        bslot = None
        def evac(oc, tc, ts, bank):
            VE = self.VE
            VE.wait("pe", self.c("pe"))
            VE.tensor_add(self.x[oc][:, ts], self.x[oc][:, ts], self.pb[bank][:, 0:512])
            VE.inc("dve")
        self.proj("l0_wo", self.sB, NIC, evac)

    def head_ln2(self, qt, kt, fillers=None):
        T, NTC = self.T, self.NTC
        TE, VE, SC = self.TE, self.VE, self.SC
        c1 = 1.0 / DH
        qs = slice(0, 512)
        ks = slice(512, 1024)
        pe_stats = {}
        rows_read = {}
        rows_done = {}
        lnexp_done = {}
        bc_done = {}
        apply_dve = {}
        entry_dve = self.c("dve")

        def emit_stats(tc):
            ts = slice(tc * 512, (tc + 1) * 512)
            qb = self.pb[2 + tc % 2]
            kb = self.pb[4 + tc % 2]
            SC.wait("dve", apply_dve.get(tc - 2, entry_dve))
            if tc >= 1:
                SC.wait("pe", pe_stats[tc - 1])
            SC.activation(self.sq0[:, 0:512], qt[:, ts], AF.Square)
            SC.activation(self.sq1[:, 0:512], kt[:, ts], AF.Square)
            SC.inc("act")
            TE.wait("act", self.c("act"))
            if tc > 1:
                TE.wait("dve", rows_read[tc - 2])
            if tc <= 1:
                TE.wait("dve", entry_dve)
            TE.matmul(qb[0:2, 0:512], self.hpat[:], qt[:, ts], start=True, stop=True)
            TE.matmul(qb[32:34, 0:512], self.hpat[:], self.sq0[:, 0:512], start=True, stop=True)
            TE.matmul(kb[0:2, 0:512], self.hpat[:], kt[:, ts], start=True, stop=True)
            TE.matmul(kb[32:34, 0:512], self.hpat[:], self.sq1[:, 0:512], start=True, stop=True)
            TE.inc("pe")
            pe_stats[tc] = self.c("pe")

        def emit_rows(tc):
            qb = self.pb[2 + tc % 2]
            kb = self.pb[4 + tc % 2]
            VE.wait("pe", pe_stats[tc])
            if tc >= 1:
                VE.wait("act", lnexp_done[tc - 1])
                VE.wait("pe", bc_done[tc - 1])
            VE.tensor_scalar_mul(self.rw0[0:2, qs], qb[0:2, 0:512], c1)
            VE.tensor_scalar_mul(self.rw1[0:2, qs], qb[32:34, 0:512], c1)
            VE.tensor_scalar_mul(self.rw0[0:2, ks], kb[0:2, 0:512], c1)
            VE.tensor_scalar_mul(self.rw1[0:2, ks], kb[32:34, 0:512], c1)
            VE.inc("dve")
            rows_read[tc] = self.c("dve")
            VE.wait("dve", self.c("dve"))
            VE.tensor_mul(self.rw2[0:2, :], self.rw0[0:2, :], self.rw0[0:2, :])
            VE.inc("dve")
            VE.wait("dve", self.c("dve"))
            VE.scalar_tensor_tensor(self.rw1[0:2, :], self.rw1[0:2, :], EPS,
                                    self.rw2[0:2, :], ALU.add, ALU.subtract)
            VE.inc("dve")
            SC.wait("dve", self.c("dve"))
            SC.activation(self.rw1[0:2, :], self.rw1[0:2, :], AF.Ln)
            SC.inc("act")
            SC.wait("act", self.c("act"))
            SC.activation(self.rw2[0:2, :], self.rw1[0:2, :], AF.Exp, scale=-0.5)
            SC.inc("act")
            lnexp_done[tc] = self.c("act")
            VE.wait("act", self.c("act"))
            VE.scalar_tensor_tensor(self.mr3[0:2, :], self.rw0[0:2, :], -1.0,
                                    self.rw2[0:2, :], ALU.mult, ALU.mult)
            VE.inc("dve")
            VE.tensor_copy(self.rstd_bf[0:2, :], self.rw2[0:2, :])
            VE.inc("dve")
            rows_done[tc] = self.c("dve")

        def emit_apply(tc):
            ts = slice(tc * 512, (tc + 1) * 512)
            for hl_pat, t_, cols in ((self.hl_q, qt, qs), (self.hl_k, kt, ks)):
                TE.wait("dve", rows_done[tc] if t_ is qt else self.c("dve"))
                TE.matmul(self.pb[0][:, 0:512], hl_pat[0:2, :], self.rstd_bf[0:2, cols],
                          start=True, stop=True)
                TE.matmul(self.pb[1][:, 0:512], hl_pat[0:3, :], self.mr3[0:3, cols],
                          start=True, stop=True)
                TE.inc("pe")
                VE.wait("pe", self.c("pe"))
                VE.tensor_tensor(self.sq0[:, 0:512] if t_ is qt else self.sq1[:, 0:512],
                                 t_[:, ts], self.pb[0][:, 0:512], ALU.mult)
                VE.inc("dve")
                VE.wait("dve", self.c("dve"))
                VE.tensor_tensor(t_[:, ts],
                                 self.sq0[:, 0:512] if t_ is qt else self.sq1[:, 0:512],
                                 self.pb[1][:, 0:512], ALU.add)
                VE.inc("dve")
            bc_done[tc] = self.c("pe")
            apply_dve[tc] = self.c("dve")

        fillers = list(fillers or [])

        def fill(n):
            for _ in range(n):
                if fillers:
                    fillers.pop(0)()

        emit_stats(0)
        if NTC > 1:
            emit_stats(1)
        emit_rows(0)
        fill(4)
        for tc in range(1, NTC):
            emit_apply(tc - 1)
            if tc + 1 < NTC:
                emit_stats(tc + 1)
            emit_rows(tc)
            fill(4)
        emit_apply(NTC - 1)
        fill(len(fillers))

    def l0_linattn(self, qt, kt, vt, pair):
        NCH = self.NCH
        TE, VE, SC = self.TE, self.VE, self.SC
        ob = self.sB[pair]
        ktr = self.ktr
        vtr = self.vtr
        tm = self.marks("dve")
        entry = self.c("dve")

        def emit_transposes(ch_lo, ch_hi):
            for ch in range(ch_lo, ch_hi):
                cs = slice(ch * 128, (ch + 1) * 128)
                tb = ch % 2
                if ch < 2:
                    TE.wait("dve", entry)
                tm.wait(TE, 2)
                TE.transpose(self.pt[tb][:, 0:128], kt[:, cs], self.ident[:])
                TE.inc("pe")
                VE.wait("pe", self.c("pe"))
                VE.tensor_copy(ktr[:, cs], self.pt[tb][:, 0:128])
                VE.inc("dve")
                tm.mark()

        def emit_kv_round(rnd, nch8):
            TE.wait("dve", self.c("dve"))
            for ch8 in range(nch8):
                ch = rnd * 8 + ch8
                for h in range(2):
                    TE.matmul(self.pb[4 + h][0:64, ch8 * 64:(ch8 + 1) * 64],
                              ktr[:, ch * 128 + 64 * h:ch * 128 + 64 * h + 64],
                              vtr[:, ch * 128 + 64 * h:ch * 128 + 64 * h + 64],
                              start=True, stop=True)
            TE.inc("pe")
            VE.wait("pe", self.c("pe"))
            for ch8 in range(nch8):
                ch = rnd * 8 + ch8
                for h in range(2):
                    hs = slice(64 * h, 64 * h + 64)
                    VE.tensor_add(self.statef[hs, :], self.statef[hs, :],
                                  self.pb[4 + h][0:64, ch8 * 64:(ch8 + 1) * 64])
                    VE.inc("dve")
                    VE.wait("dve", self.c("dve"))
                    VE.tensor_copy(self.spref[hs, ch * 64:(ch + 1) * 64], self.statef[hs, :])
                    VE.inc("dve")

        VE.wait("pe", self.c("pe"))
        VE.memset(self.statef[:], 0.0)
        VE.inc("dve")
        if NCH > 8:
            emit_transposes(0, 8)
            emit_kv_round(0, 8)
            emit_transposes(8, NCH)
            emit_kv_round(1, NCH - 8)
        else:
            emit_transposes(0, NCH)
            emit_kv_round(0, NCH)
        spref_done = self.c("dve")
        steps = [(ch, h) for ch in range(NCH) for h in range(2)]
        ns = len(steps)
        s_pe = {}
        msk_dve = {}
        o_pe = {}
        evac_dve = {}
        evac_act = {}
        mskts = [self.msk0[:, 0:128], self.sq1[:, 0:128], self.sq0[:, 0:128], self.sq1[:, 128:256]]

        def emit_s(i):
            ch, h = steps[i]
            cs = slice(ch * 128, (ch + 1) * 128)
            hs = slice(64 * h, 64 * h + 64)
            if i >= 2:
                TE.wait("dve", msk_dve[i - 2])
            TE.matmul(self.pb[i % 2][:, 0:128], kt[hs, cs], qt[hs, cs], start=True, stop=True)
            TE.inc("pe")
            s_pe[i] = self.c("pe")

        def emit_mask(i):
            VE.wait("pe", s_pe[i])
            if i >= 4:
                VE.wait("pe", o_pe[i - 4])
            VE.tensor_tensor(mskts[i % 4], self.pb[i % 2][:, 0:128], self.maskUT[:], ALU.mult)
            VE.inc("dve")
            msk_dve[i] = self.c("dve")

        def emit_o(i):
            ch, h = steps[i]
            cs = slice(ch * 128, (ch + 1) * 128)
            hs = slice(64 * h, 64 * h + 64)
            if i >= 2:
                TE.wait("dve", evac_dve[i - 2])
                TE.wait("act", evac_act[i - 2])
            TE.wait("dve", msk_dve[i])
            TE.matmul(self.pb[2 + i % 2][0:64, 0:128],
                      vtr[:, ch * 128 + 64 * h:ch * 128 + 64 * h + 64],
                      mskts[i % 4], start=True, stop=True)
            if ch > 0:
                TE.matmul(self.pb[4 + i % 2][0:64, 0:128],
                          self.spref[hs, (ch - 1) * 64:ch * 64],
                          qt[hs, cs], start=True, stop=True)
            TE.inc("pe")
            o_pe[i] = self.c("pe")

        def emit_evac(i):
            ch, h = steps[i]
            cs = slice(ch * 128, (ch + 1) * 128)
            hs = slice(64 * h, 64 * h + 64)
            if ch > 0:
                SC.wait("pe", o_pe[i])
                SC.wait("dve", self.c("dve"))
                SC.copy(self.otmp[0:64, 0:128], self.pb[4 + i % 2][0:64, 0:128])
                SC.inc("act")
                evac_act[i] = self.c("act")
                VE.wait("act", self.c("act"))
                VE.tensor_add(ob[hs, cs], self.pb[2 + i % 2][0:64, 0:128],
                              self.otmp[0:64, 0:128])
                VE.inc("dve")
                evac_dve[i] = self.c("dve")
            else:
                VE.wait("pe", o_pe[i])
                VE.tensor_copy(ob[hs, cs], self.pb[2 + i % 2][0:64, 0:128])
                VE.inc("dve")
                evac_dve[i] = self.c("dve")
                evac_act[i] = self.c("act")

        TE.wait("dve", spref_done)
        emit_s(0)
        emit_s(1)
        emit_mask(0)
        for i in range(ns - 2):
            emit_o(i)
            emit_s(i + 2)
            emit_mask(i + 1)
            emit_evac(i)
        emit_o(ns - 2)
        emit_mask(ns - 1)
        emit_evac(ns - 2)
        emit_o(ns - 1)
        emit_evac(ns - 1)

    def ph_precompute_A(self):
        nc = self.nc
        SY, GP, TE, VE = self.SY, self.GP, self.TE, self.VE
        if True:
            fc0t = self.salloc("fc0t", [128, 8192], BF16)
            fc1t = self.salloc("fc1t", [128, 8192], BF16)
            fc0 = [fc0t[:, i * 1024:(i + 1) * 1024] for i in range(NIC)]
            fc1 = [fc1t[:, i * 1024:(i + 1) * 1024] for i in range(NIC)]
            pq = [self.salloc(f"pq_{i}", [128, 2048], BF16) for i in range(NIC)]
            for pas in range(2):
                if pas == 0:
                    GP.dma_start(out=fc0t[:], in_=self.dram["c_FcrP"])
                    GP.inc("wld", 16)
                    GP.dma_start(out=fc1t[:], in_=self.dram["c_negFciP"])
                    GP.inc("wld", 16)
                else:
                    GP.wait("pe", self.c("pe"))
                    GP.dma_start(out=fc1t[:], in_=self.dram["c_FciP"])
                    GP.inc("wld", 16)
                GP.milestone("pool", "wld", self.c("wld"))
                fc_ready = self.c("pool")
                dsto = 0 if pas == 0 else 1024
                w0, w1 = (fc0, fc1) if pas == 0 else (fc1, fc0)
                for oc in range(NIC):
                    pb_ = oc % 2
                    wr_, wi_ = self.aslot(4096 * pb_, 1024), self.aslot(4096 * pb_ + 1024, 1024)
                    ws = "wld0" if oc % 2 == 0 else "wld1"
                    self.wload((4096 * pb_, 2048), self.aslot(4096 * pb_, 2048),
                               self.dram["l0_wri"][oc], ws)
                    wld_here = (ws, self.c(ws))
                    for jt in range(2):
                        js = slice(jt * 512, (jt + 1) * 512)
                        bank = jt
                        TE.wait("pool", fc_ready)
                        TE.wait(*wld_here)
                        TE.wait("dve", self.c("dve"))
                        for ic in range(NIC):
                            TE.matmul(self.pb[bank][:, 0:512], wr_[:, ic * 128:(ic + 1) * 128],
                                      (fc0 if pas == 0 else fc1)[ic][:, js],
                                      start=(ic == 0), stop=False)
                        for ic in range(NIC):
                            TE.matmul(self.pb[bank][:, 0:512], wi_[:, ic * 128:(ic + 1) * 128],
                                      (fc1 if pas == 0 else fc0)[ic][:, js],
                                      start=False, stop=(ic == NIC - 1))
                        TE.inc("pe")
                        VE.wait("pe", self.c("pe"))
                        VE.tensor_copy(pq[oc][:, dsto + jt * 512:dsto + (jt + 1) * 512],
                                       self.pb[bank][:, 0:512])
                        VE.inc("dve")
                    self.wdone((4096 * pb_, 2048))
            stage1_dve = self.c("dve")
            for oc in range(NIC):
                pb_ = oc % 2
                fr_, fi_ = self.aslot(4096 * pb_, 1024), self.aslot(4096 * pb_ + 1024, 1024)
                ws = "wld0" if oc % 2 == 0 else "wld1"
                self.wload((4096 * pb_, 2048), self.aslot(4096 * pb_, 2048),
                           self.dram["c_FrFi"][oc], ws)
                wld_here = (ws, self.c(ws))
                for jt in range(2):
                    js = slice(jt * 512, (jt + 1) * 512)
                    bank = jt
                    TE.wait(*wld_here)
                    if oc == 0 and jt == 0:
                        TE.wait("dve", stage1_dve)
                    TE.wait("pool", max(0, self.c("pool") - 1))
                    for ic in range(NIC):
                        TE.matmul(self.pb[bank][:, 0:512], fr_[:, ic * 128:(ic + 1) * 128],
                                  pq[ic][:, js], start=(ic == 0), stop=False)
                    for ic in range(NIC):
                        TE.matmul(self.pb[bank][:, 0:512], fi_[:, ic * 128:(ic + 1) * 128],
                                  pq[ic][:, 1024 + jt * 512:1024 + (jt + 1) * 512],
                                  start=False, stop=(ic == NIC - 1))
                    TE.inc("pe")
                    VE.wait("pe", self.c("pe"))
                    VE.tensor_copy((self.sq0 if jt == 0 else self.sq1)[:, 0:512],
                                   self.pb[bank][:, 0:512])
                    VE.inc("dve")
                    SY.wait("dve", self.c("dve"))
                    SY.dma_start(out=self.A_dram[:, oc * 1024 + jt * 512:oc * 1024 + (jt + 1) * 512],
                                 in_=(self.sq0 if jt == 0 else self.sq1)[:, 0:512])
                    SY.inc("st", 16)
                    SY.milestone("pool", "st", self.c("st"))
                self.wdone((4096 * pb_, 2048))
            TE.wait("pool", self.c("pool"))
            VE.wait("pool", self.c("pool"))
            self.SC.wait("pool", self.c("pool"))

    def ph_l0_ffn(self):
        T, NTC = self.T, self.NTC
        GP, TE, VE = self.GP, self.TE, self.VE
        SY = self.SY
        VE.wait("dve", self.c("dve"))
        VE.memset(self.zrow[:], 0.0)
        VE.inc("dve")
        SY.wait("dve", self.c("dve"))
        for e in range(8):
            for r in range(8):
                for hh in range(2):
                    SY.dma_start(
                        out=self.hsort[e * 2048 + r * 128:e * 2048 + (r + 1) * 128,
                                       hh * 512:(hh + 1) * 512],
                        in_=self.zrow[:])
                    SY.inc("hz", 16)
        for r in range(128, 192):
            for hh in range(2):
                SY.dma_start(out=self.hsort[r * 128:(r + 1) * 128, hh * 512:(hh + 1) * 512],
                             in_=self.zrow[:])
                SY.inc("hz", 16)
        for r in range(128, 193):
            for hh in range(2):
                SY.dma_start(out=self.osort[r * 128:(r + 1) * 128, hh * 512:(hh + 1) * 512],
                             in_=self.zrow[:])
                SY.inc("hz", 16)
        self.hz_zero = self.c("hz")
        self.ln(self.x, self.sA, self.pat_plain, src_f32=True)
        GP.wait("pool", self.c("pool"))
        self.wload((0, 8192), self.aslot(0, 8192), self.A_dram, "wld")
        GP.milestone("pool", "wld", self.c("wld"))
        pool_A = self.c("pool")
        bofs = self.bslot["l0ffn"]
        am = self.marks("dve")
        for oc in range(NIC):
            for tc in range(NTC):
                ts = slice(tc * 512, (tc + 1) * 512)
                bank = (oc * NTC + tc) % 4
                if oc == 0 and tc == 0:
                    TE.wait("pool", pool_A)
                    TE.wait("dve", self.c("dve"))
                am.wait(TE, 4)
                for ic in range(NIC):
                    TE.matmul(self.pb[bank][:, 0:512],
                              self.aslot(ic * 1024, 1024)[:, oc * 128:(oc + 1) * 128],
                              self.sA[ic][:, ts], start=(ic == 0), stop=(ic == NIC - 1))
                TE.inc("pe")
                VE.wait("pe", self.c("pe"))
                VE.scalar_tensor_tensor(self.x[oc][:, ts], self.pb[bank][:, 0:512],
                                        self.bias[:, bofs + oc:bofs + oc + 1],
                                        self.x[oc][:, ts], ALU.add, ALU.add)
                VE.inc("dve")
                am.mark()
        self.wdone((0, 8192))

    def mingru_layer(self, wz_name, wh_name, lidx, src, dst, bz_slot, bh_slot, rev):
        T, NTC = self.T, self.NTC
        GP, TE, VE, SC = self.GP, self.TE, self.VE, self.SC
        mgm = self.marks("dve")
        stage_dve = {}
        entry_dve = self.c("dve")
        zs = [self.otmp[:, 0:512], self.zst2[:, 0:512]]
        ts_ = [self.tmp2[:, 0:512], self.tst2[:, 0:512]]
        for oc in range(NIC):
            gr = oc // 2
            pb_ = gr % 2
            base = 4096 * pb_ + (oc % 2) * 2048
            wz_, wh_ = self.aslot(base, 1024), self.aslot(base + 1024, 1024)
            ws = "wld0" if pb_ == 0 else "wld1"
            if oc % 2 == 0:
                self.wload((4096 * pb_, 4096), self.aslot(4096 * pb_, 4096),
                           self.dram[wz_name][lidx * 4 + gr], ws)
            wld_here = (ws, self.c(ws))
            for tc in range(NTC):
                tsl = slice(tc * 512, (tc + 1) * 512)
                idx = oc * NTC + tc
                b0, b1 = 2 * (idx % 3), 2 * (idx % 3) + 1
                zt_, tt_ = zs[idx % 2], ts_[idx % 2]
                if tc == 0:
                    TE.wait(*wld_here)
                    if oc == 0:
                        TE.wait("dve", entry_dve)
                mgm.wait(TE, 3)
                for ic in range(NIC):
                    TE.matmul(self.pb[b0][:, 0:512], wz_[:, ic * 128:(ic + 1) * 128],
                              src[ic][:, tsl], start=(ic == 0), stop=(ic == NIC - 1))
                TE.inc("pe")
                for ic in range(NIC):
                    TE.matmul(self.pb[b1][:, 0:512], wh_[:, ic * 128:(ic + 1) * 128],
                              src[ic][:, tsl], start=(ic == 0), stop=(ic == NIC - 1))
                TE.inc("pe")
                SC.wait("pe", self.c("pe"))
                SC.wait("dve", stage_dve.get(idx - 2, entry_dve))
                SC.activation(zt_, self.pb[b0][:, 0:512], AF.Sigmoid,
                              bias=self.bias[:, bz_slot + oc:bz_slot + oc + 1])
                SC.inc("act")
                SC.activation(tt_, self.pb[b1][:, 0:512], AF.Tanh,
                              bias=self.bias[:, bh_slot + oc:bh_slot + oc + 1])
                SC.inc("act")
                VE.wait("act", self.c("act"))
                VE.wait("dve", self.c("dve"))
                VE.tensor_scalar(self.scr0[:, tsl], zt_, -1.0, 1.0, ALU.mult, ALU.add)
                VE.inc("dve")
                VE.wait("dve", self.c("dve"))
                VE.tensor_scalar(self.scr0[:, tsl], self.scr0[:, tsl], 1e-8, 1.0 - 1e-8,
                                 ALU.max, ALU.min)
                VE.inc("dve")
                VE.tensor_mul(self.bbf[:, tsl], zt_, tt_)
                VE.inc("dve")
                stage_dve[idx] = self.c("dve")
                mgm.mark()
            VE.wait("dve", self.c("dve"))
            if rev:
                VE.tensor_tensor_scan(dst[oc][:, ::-1], self.scr0[:, ::-1],
                                      self.bbf[:, ::-1], 0.0, ALU.mult, ALU.add)
            else:
                VE.tensor_tensor_scan(dst[oc][:], self.scr0[:], self.bbf[:],
                                      0.0, ALU.mult, ALU.add)
            VE.inc("dve")
            if lidx > 0:
                VE.wait("dve", self.c("dve"))
                VE.tensor_add(dst[oc][:], dst[oc][:], src[oc][:])
                VE.inc("dve")
            if oc % 2 == 1:
                self.wdone((4096 * pb_, 4096))
        self.ln(dst, dst, self.pat_l1a, src_f32=False)

    def ph_l1_attn(self):
        T, NTC = self.T, self.NTC
        GP, TE, VE, SC = self.GP, self.TE, self.VE, self.SC
        self.scr0 = self.salloc("scr0", [128, T], BF16)
        self.bbf = self.salloc("bbf", [128, T], BF16)
        self.zst2 = self.salloc("zst2", [128, 512], BF16)
        self.tst2 = self.salloc("tst2", [128, 512], BF16)
        self.ln(self.x, self.sA, self.pat_plain, src_f32=True)
        bufs = [self.sA, self.sB]
        for l in range(4):
            self.mingru_layer("fwd_wzh", None, l, bufs[l % 2], bufs[(l + 1) % 2],
                              self.bslot["fwd_bz"] + l * 8, self.bslot["fwd_bh"] + l * 8,
                              rev=False)
        self.ln(self.x, self.sB, self.pat_plain, src_f32=True)
        fofs = self.bslot["fus"]

        def evac_f(oc, tc, ts, bank):
            VE = self.VE
            VE.wait("pe", self.c("pe"))
            VE.scalar_tensor_tensor(self.x[oc][:, ts], self.pb[bank][:, 0:512],
                                    self.bias[:, fofs + oc:fofs + oc + 1],
                                    self.x[oc][:, ts], ALU.add, ALU.add)
            VE.inc("dve")
        self.proj("fus_top", self.sA, NIC, evac_f)
        bufs = [self.sB, self.sA]
        for l in range(4):
            self.mingru_layer("bwd_wzh", None, l, bufs[l % 2], bufs[(l + 1) % 2],
                              self.bslot["bwd_bz"] + l * 8, self.bslot["bwd_bh"] + l * 8,
                              rev=True)

        def evac_b(oc, tc, ts, bank):
            VE = self.VE
            VE.wait("pe", self.c("pe"))
            VE.tensor_add(self.x[oc][:, ts], self.x[oc][:, ts], self.pb[bank][:, 0:512])
            VE.inc("dve")
        self.proj("fus_bot", self.sB, NIC, evac_b)

    def ph_l1_ffn_attn(self):
        T, NTC, NCH = self.T, self.NTC, self.NCH
        GP, TE, VE, SC, SY = self.GP, self.TE, self.VE, self.SC, self.SY
        self.ln(self.x, self.sA, self.pat_ln1, src_f32=True)
        vt = [self.salloc(f"vt{i}", [128, 130], BF16) for i in range(self.NCH)]
        for i in range(self.NCH):
            VE.memset(vt[i][:, 64:65], 1.0)
            VE.memset(vt[i][:, 129:130], 1.0)
        VE.engine_nop()
        VE.inc("dve")
        qofs = self.bslot["bqkv"]
        for pair in range(NPAIR):
            pb_ = pair % 2
            wq, wk, wv = (self.aslot(3072 * pb_, 1024), self.aslot(3072 * pb_ + 1024, 1024),
                          self.aslot(3072 * pb_ + 2048, 1024))
            qt, kt, vtile = self.sC[0], self.sC[1], self.sC[2]
            ws = "wld0" if pb_ == 0 else "wld1"
            if pair >= 1:
                GP.wait("dve", self.c("dve"))
            self.wload((3072 * pb_, 3072), self.aslot(3072 * pb_, 3072), self.dram["wqkv3"][pair], ws)
            wld_here = (ws, self.c(ws))
            pm = self.marks("dve")
            for which, (wt, dst, bofs) in enumerate(
                    [(wq, qt, qofs + pair), (wk, kt, qofs + 8 + pair),
                     (wv, vtile, qofs + 16 + pair)]):
                for tc in range(NTC):
                    ts = slice(tc * 512, (tc + 1) * 512)
                    bank = (which * NTC + tc) % 4
                    if which == 0 and tc == 0:
                        TE.wait(*wld_here)
                        TE.wait("dve", self.c("dve"))
                    pm.wait(TE, 4)
                    for ic in range(NIC):
                        TE.matmul(self.pb[bank][:, 0:512], wt[:, ic * 128:(ic + 1) * 128],
                                  self.sA[ic][:, ts], start=(ic == 0), stop=(ic == NIC - 1))
                    TE.inc("pe")
                    VE.wait("pe", self.c("pe"))
                    VE.tensor_scalar(dst[:, ts], self.pb[bank][:, 0:512],
                                     self.bias[:, bofs:bofs + 1], None, ALU.add)
                    VE.inc("dve")
                    pm.mark()
            self.wdone((3072 * pb_, 3072))
            c_steps = []
            for which, t_ in ((0, qt), (1, kt)):
                for tc in range(NTC):
                    def cstep(which=which, t_=t_, tc=tc):
                        ts = slice(tc * 512, (tc + 1) * 512)
                        SC.wait("dve", self.c("dve"))
                        SC.wait("pe", self.c("pe"))
                        SC.activation(self.sq0[:, 0:512], t_[:, ts], AF.Square)
                        SC.inc("act")
                        TE.wait("act", self.c("act"))
                        TE.wait("dve", self.c("dve"))
                        TE.matmul(self.pb[5][0:1, 0:512], self.ones_col[:, 0:1],
                                  self.sq0[:, 0:512], start=True, stop=True)
                        TE.inc("pe")
                        VE.wait("pe", self.c("pe"))
                        if tc == 0:
                            VE.reduce_max(self.rw0[0:1, which:which + 1],
                                          self.pb[5][0:1, 0:512], axis=AX.X)
                            VE.inc("dve")
                        else:
                            VE.reduce_max(self.rw0[0:1, 2:3], self.pb[5][0:1, 0:512], axis=AX.X)
                            VE.inc("dve")
                            VE.wait("dve", self.c("dve"))
                            VE.tensor_max(self.rw0[0:1, which:which + 1],
                                          self.rw0[0:1, which:which + 1], self.rw0[0:1, 2:3])
                            VE.inc("dve")
                    c_steps.append(cstep)
            ci = 0
            for kc in range(NCH):
                cs = slice(kc * 128, (kc + 1) * 128)
                tb = kc % 2
                TE.wait("dve", self.c("dve"))
                TE.transpose(self.pt[tb][:, 0:128], vtile[:, cs], self.ident[:])
                TE.inc("pe")
                VE.wait("pe", self.c("pe"))
                VE.tensor_copy(vt[kc][:, 0:64], self.pt[tb][:, 0:64])
                VE.tensor_copy(vt[kc][:, 65:129], self.pt[tb][:, 64:128])
                VE.inc("dve")
                if kc % 2 == 1 and ci < len(c_steps):
                    c_steps[ci]()
                    ci += 1
            while ci < len(c_steps):
                c_steps[ci]()
                ci += 1
            SC.wait("dve", self.c("dve"))
            SC.activation(self.rw1[0:1, 0:2], self.rw0[0:1, 0:2], AF.Ln)
            SC.inc("act")
            VE.wait("act", self.c("act"))
            VE.tensor_add(self.rw0[0:1, 3:4], self.rw1[0:1, 0:1], self.rw1[0:1, 1:2])
            VE.inc("dve")
            SC.wait("dve", self.c("dve"))
            SC.activation(self.rw0[0:1, 4:5], self.rw0[0:1, 3:4], AF.Exp, scale=0.5)
            SC.inc("act")
            VE.wait("act", self.c("act"))
            VE.tensor_scalar_mul(self.c1b[0:1, 0:1], self.rw0[0:1, 4:5], -0.125)
            VE.inc("dve")
            TE.wait("dve", self.c("dve"))
            TE.matmul(self.pb[5][:, 0:1], self.ones_row[0:1, :], self.c1b[0:1, 0:1],
                      start=True, stop=True)
            TE.inc("pe")
            VE.wait("pe", self.c("pe"))
            VE.tensor_copy(self.ccol[:, 0:1], self.pb[5][:, 0:1])
            VE.inc("dve")
            ccol_ready0 = self.c("dve")
            for h in range(2):
                hs = slice(64 * h, 64 * h + 64)
                ccol_ready = ccol_ready0
                denom_pending = []

                def emit_denom(tc):
                    ts2 = slice(tc * 512, (tc + 1) * 512)
                    pvb = self.pb[4 if tc % 2 == 0 else 2]
                    bcb = self.pb[5 if tc % 2 == 0 else 3]
                    VE.wait("pe", pv_done[tc])
                    VE.reciprocal(self.crow[0:1, 0:512], pvb[64:65, 0:512])
                    VE.inc("dve")
                    VE.wait("dve", self.c("dve"))
                    VE.tensor_copy(self.rstd_bf[0:1, 0:512], self.crow[0:1, 0:512])
                    VE.inc("dve")
                    TE.wait("dve", self.c("dve"))
                    TE.matmul(bcb[0:64, 0:512], self.ones_row[0:1, 0:64],
                              self.rstd_bf[0:1, 0:512], start=True, stop=True)
                    TE.inc("pe")
                    SC.wait("pe", self.c("pe"))
                    SC.wait("dve", self.c("dve"))
                    SC.copy(self.otmp[0:64, 0:512], pvb[0:64, 0:512])
                    SC.inc("act")
                    VE.wait("act", self.c("act"))
                    VE.tensor_tensor(self.sB[pair][hs, ts2], self.otmp[0:64, 0:512],
                                     bcb[0:64, 0:512], ALU.mult)
                    VE.inc("dve")

                pv_done = {}
                for tc in range(NTC):
                    ts = slice(tc * 512, (tc + 1) * 512)
                    pvb = self.pb[4 if tc % 2 == 0 else 2]
                    s_marks = []
                    pv_marks = []
                    exp_marks = []

                    def emit_S(kc):
                        cs = slice(kc * 128, (kc + 1) * 128)
                        bank = kc % 2
                        if kc >= 2:
                            TE.wait("act", exp_marks[kc - 2])
                        TE.matmul(self.pb[bank][:, 0:512], kt[hs, cs], qt[hs, ts],
                                  start=True, stop=True)
                        TE.inc("pe")
                        s_marks.append(self.c("pe"))

                    def emit_exp(kc):
                        bank = kc % 2
                        ptile = self.sq0 if kc % 2 == 0 else self.sq1
                        SC.wait("pe", s_marks[kc])
                        if kc == 0:
                            SC.wait("dve", ccol_ready)
                        if kc >= 2:
                            SC.wait("pe", pv_marks[kc - 2])
                        SC.activation(ptile[:, 0:512], self.pb[bank][:, 0:512], AF.Exp,
                                      scale=0.125, bias=self.ccol[:, 0:1])
                        SC.inc("act")
                        exp_marks.append(self.c("act"))

                    def emit_PV(kc):
                        ptile = self.sq0 if kc % 2 == 0 else self.sq1
                        TE.wait("act", exp_marks[kc])
                        if kc == 0 and tc >= 2:
                            TE.wait("dve", self.c("dve"))
                        TE.matmul(pvb[0:65, 0:512], vt[kc][:, 65 * h:65 * h + 65],
                                  ptile[:, 0:512], start=(kc == 0), stop=(kc == NCH - 1))
                        TE.inc("pe")
                        pv_marks.append(self.c("pe"))

                    emit_S(0)
                    emit_S(1)
                    emit_exp(0)
                    for kc in range(NCH - 2):
                        emit_PV(kc)
                        emit_S(kc + 2)
                        emit_exp(kc + 1)
                    emit_PV(NCH - 2)
                    emit_exp(NCH - 1)
                    emit_PV(NCH - 1)
                    pv_done[tc] = self.c("pe")
                    if denom_pending:
                        emit_denom(denom_pending.pop())
                    denom_pending.append(tc)
                if denom_pending:
                    emit_denom(denom_pending.pop())
        bofs = self.bslot["bo"]

        def evac_o(oc, tc, ts, bank):
            VE = self.VE
            VE.wait("pe", self.c("pe"))
            VE.scalar_tensor_tensor(self.x[oc][:, ts], self.pb[bank][:, 0:512],
                                    self.bias[:, bofs + oc:bofs + oc + 1],
                                    self.x[oc][:, ts], ALU.add, ALU.add)
            VE.inc("dve")
        self.proj("wo2", self.sB, NIC, evac_o)
        self.bar()
        self.ln(self.x, self.sB, self.pat_ln2, src_f32=True)
        for ic in range(NIC):
            GP.dma_start(out=self.gatew[:, ic * 8:(ic + 1) * 8],
                         in_=self.dram["gate_pack"][ic])
            GP.inc("wld", 16)
        GP.milestone("pool", "wld", self.c("wld"))
        gw_ready = self.c("pool")
        for tc in range(NTC):
            ts = slice(tc * 512, (tc + 1) * 512)
            TE.wait("pool", gw_ready)
            TE.wait("dve", self.c("dve"))
            for ic in range(NIC):
                TE.matmul(self.pb[0][0:8, 0:512], self.gatew[:, ic * 8:(ic + 1) * 8],
                          self.sB[ic][:, ts], start=(ic == 0), stop=(ic == NIC - 1))
            TE.inc("pe")
            VE.wait("pe", self.c("pe"))
            VE.tensor_scalar(self.sq0[0:8, 0:512], self.pb[0][0:8, 0:512],
                             self.gateb[:, 0:1], None, ALU.add)
            VE.inc("dve")
            for c4 in range(4):
                cs = slice(c4 * 128, (c4 + 1) * 128)
                chunk = tc * 4 + c4
                TE.wait("dve", self.c("dve"))
                TE.transpose(self.pt[0][:, 0:8], self.sq0[0:8, c4 * 128:(c4 + 1) * 128],
                             self.ident[0:8, 0:8])
                TE.inc("pe")
                VE.wait("pe", self.c("pe"))
                VE.tensor_copy(self.otmp[:, 0:8], self.pt[0][:, 0:8])
                VE.inc("dve")
                VE.wait("dve", self.c("dve"))
                VE.reduce_max(self.otmp[:, 8:9], self.otmp[:, 0:8], axis=AX.X)
                VE.inc("dve")
                VE.wait("dve", self.c("dve"))
                VE.tensor_scalar(self.otmp[:, 16:24], self.otmp[:, 0:8],
                                 self.otmp[:, 8:9], None, ALU.is_equal)
                VE.inc("dve")
                VE.wait("dve", self.c("dve"))
                VE.scalar_tensor_tensor(self.otmp[:, 24:32], self.otmp[:, 16:24], -1e30,
                                        self.otmp[:, 0:8], ALU.mult, ALU.add)
                VE.inc("dve")
                VE.wait("dve", self.c("dve"))
                VE.reduce_max(self.otmp[:, 9:10], self.otmp[:, 24:32], axis=AX.X)
                VE.inc("dve")
                VE.wait("dve", self.c("dve"))
                VE.tensor_scalar(self.otmp[:, 32:40], self.otmp[:, 24:32],
                                 self.otmp[:, 9:10], None, ALU.is_equal)
                VE.inc("dve")
                VE.wait("dve", self.c("dve"))
                VE.tensor_sub(self.otmp[:, 10:11], self.otmp[:, 8:9], self.otmp[:, 9:10])
                VE.inc("dve")
                SC.wait("dve", self.c("dve"))
                SC.activation(self.otmp[:, 11:12], self.otmp[:, 10:11], AF.Sigmoid)
                SC.inc("act")
                VE.wait("act", self.c("act"))
                VE.tensor_scalar(self.otmp[:, 12:13], self.otmp[:, 11:12], -1.0, 1.0,
                                 ALU.mult, ALU.add)
                VE.inc("dve")
                VE.wait("dve", self.c("dve"))
                VE.tensor_scalar_mul(self.otmp[:, 16:24], self.otmp[:, 16:24],
                                     self.otmp[:, 11:12])
                VE.inc("dve")
                VE.wait("dve", self.c("dve"))
                VE.tensor_scalar_mul(self.otmp[:, 32:40], self.otmp[:, 32:40],
                                     self.otmp[:, 12:13])
                VE.inc("dve")
                VE.wait("dve", self.c("dve"))
                VE.tensor_add(self.otmp[:, 16:24], self.otmp[:, 16:24], self.otmp[:, 32:40])
                VE.inc("dve")
                VE.wait("dve", self.c("dve"))
                VE.tensor_copy(self.sq1[:, 0:8], self.otmp[:, 16:24])
                VE.inc("dve")
                TE.wait("dve", self.c("dve"))
                TE.transpose(self.pt[1][0:8, 0:128], self.sq1[:, 0:8], self.ident[:])
                TE.inc("pe")
                VE.wait("pe", self.c("pe"))
                VE.tensor_copy(self.ew8[0:8, chunk * 128:(chunk + 1) * 128],
                               self.pt[1][0:8, 0:128])
                VE.inc("dve")

    def ph_compact(self):
        T = self.T
        GP, TE, VE, SC, SY = self.GP, self.TE, self.VE, self.SC, self.SY
        pos = self.salloc("cp_pos", [8, T], F32)[0:8, :]
        slotm = self.salloc("cp_slotm", [8, T], F32)[0:8, :]
        mfp = self.sA[3][0:8, :]
        one8 = self.sA[2][0:8, :]
        markt = self.sA[4][0:8, :]
        isov = self.sA[5][0:8, :]
        i1T = self.salloc("cp_i1T", [128, 16], F32)
        i2T = self.salloc("cp_i2T", [128, 16], F32)
        waT = self.salloc("cp_waT", [128, 16], F32)
        wbT = self.salloc("cp_wbT", [128, 16], F32)
        slT = self.salloc("cp_slT", [128, 8], F32)
        ewTt = self.salloc("cp_ewTt", [128, 8], BF16)
        eqt = self.salloc("cp_eqt", [128, 8], F32)
        t2t = self.salloc("cp_t2t", [128, 8], F32)
        wsumc = self.salloc("cp_wsum", [128, 1], F32)
        cmp8 = self.salloc("cp_cmp8", [8, 16], F32)
        cmpb = self.salloc("cp_cmpb", [8, 16], BF16)
        teb = self.salloc("cp_teb", [1, 16], BF16)
        wqf = self.salloc("cp_wqf", [128, 16], F32)
        wqg = self.salloc("cp_wqg", [128, 16], F32)
        prestg = [self.sA[0][:, 0:1024], self.sA[1][:, 0:1024]]
        wab = self.salloc("cp_wab", [128, 32], F32)
        scat = self.arena[:, 0:8192].rearrange("p (a b) -> p a b", b=1024)
        self.identf = self.salloc("cp_identf", [128, 128], F32)
        GP.dma_start(out=self.identf[:], in_=self.dram["c_identf"])
        GP.inc("wld", 16)
        idf_wld = self.c("wld")
        TE.wait("wld", idf_wld)
        VE.wait("dve", self.c("dve"))
        VE.memset(one8, 1.0)
        VE.inc("dve")
        VE.tensor_scalar(mfp, self.ew8[0:8, :], 0.0, None, ALU.not_equal)
        VE.inc("dve")
        VE.wait("dve", self.c("dve"))
        VE.wait("dve", self.c("dve"))
        VE.tensor_tensor_scan(pos, one8, mfp, 0.0, ALU.mult, ALU.add)
        VE.inc("dve")
        VE.wait("dve", self.c("dve"))
        VE.tensor_scalar(self.ccol[0:8, 2:3], pos[:, T - 1:T], 1025.0, None, ALU.is_ge)
        VE.inc("dve")
        TE.wait("dve", self.c("dve"))
        TE.transpose(self.pb[0][0:1, 0:8], self.ccol[0:8, 2:3], self.identf[0:8, 0:8])
        TE.inc("pe")
        VE.wait("pe", self.c("pe"))
        VE.tensor_copy(self.rw0[0:1, 0:8], self.pb[0][0:1, 0:8])
        VE.inc("dve")
        VE.wait("dve", self.c("dve"))
        VE.tensor_tensor_scan(self.rw0[0:1, 8:16], self.ones_row[0:1, 0:8],
                              self.rw0[0:1, 0:8], 0.0, ALU.mult, ALU.add)
        VE.inc("dve")
        VE.wait("dve", self.c("dve"))
        VE.tensor_sub(self.rw0[0:1, 16:24], self.rw0[0:1, 8:16], self.rw0[0:1, 0:8])
        VE.inc("dve")
        TE.wait("dve", self.c("dve"))
        TE.transpose(self.pb[1][0:8, 0:1], self.rw0[0:1, 16:24], self.identf[0:1, 0:1])
        TE.inc("pe")
        VE.wait("pe", self.c("pe"))
        VE.tensor_copy(self.ccol[0:8, 3:4], self.pb[1][0:8, 0:1])
        VE.inc("dve")
        VE.wait("dve", self.c("dve"))
        VE.tensor_scalar(self.ccol[0:8, 5:6], self.ccol[0:8, 2:3], -99.0, 99.0,
                         ALU.mult, ALU.add)
        VE.inc("dve")
        VE.wait("dve", self.c("dve"))
        VE.tensor_add(self.ccol[0:8, 5:6], self.ccol[0:8, 5:6], self.ccol[0:8, 3:4])
        VE.inc("dve")
        VE.wait("dve", self.c("dve"))
        VE.tensor_scalar(cmp8[0:8, 0:3], self.irow[0:8, 0:3], self.ccol[0:8, 5:6],
                         None, ALU.is_ge)
        VE.inc("dve")
        VE.wait("dve", self.c("dve"))
        VE.tensor_copy(cmpb[0:8, 0:3], cmp8[0:8, 0:3])
        VE.inc("dve")
        TE.wait("dve", self.c("dve"))
        TE.matmul(self.pb[2][0:1, 0:3], self.ones_col[0:8, 0:1], cmpb[0:8, 0:3],
                  start=True, stop=True)
        TE.inc("pe")
        VE.wait("pe", self.c("pe"))
        VE.tensor_scalar(self.rw0[0:1, 32:35], self.pb[2][0:1, 0:3], -1.0, None, ALU.add)
        VE.inc("dve")
        VE.wait("dve", self.c("dve"))
        VE.tensor_scalar_min(self.rw0[0:1, 32:35], self.rw0[0:1, 32:35], 7.0)
        VE.inc("dve")
        VE.wait("dve", self.c("dve"))
        VE.tensor_scalar(self.rw0[0:1, 36:39], self.rw0[0:1, 32:35], 0.0, None, ALU.max)
        VE.inc("dve")
        VE.wait("dve", self.c("dve"))
        VE.tensor_copy(teb[0:1, 0:3], self.rw0[0:1, 36:39])
        VE.inc("dve")
        TE.wait("dve", self.c("dve"))
        TE.matmul(self.pb[3][:, 0:3], self.ones_row[0:1, :], teb[0:1, 0:3],
                  start=True, stop=True)
        TE.inc("pe")
        VE.wait("pe", self.c("pe"))
        VE.tensor_scalar(wqf[:, 0:3], self.pb[3][:, 0:3], 512.0, None, ALU.mult)
        VE.inc("dve")
        for g in range(4):
            VE.wait("dve", self.c("dve"))
            VE.tensor_scalar(wqg[:, 0:3], wqf[:, 0:3], self.pgt[:, g:g + 1],
                             None, ALU.add)
            VE.inc("dve")
            VE.wait("dve", self.c("dve"))
            VE.tensor_copy(self.wqidx[:, g * 3:(g + 1) * 3], wqg[:, 0:3])
            VE.inc("dve")
        ovf = self.salloc("cp_ovf", [128, 64], F32)
        for k in range(3):
            VE.wait("dve", self.c("dve"))
            VE.tensor_scalar(wqg[:, 4:5], self.pb[3][:, k:k + 1], 1024.0, 16384.0,
                             ALU.mult, ALU.add)
            VE.inc("dve")
            VE.wait("dve", self.c("dve"))
            VE.tensor_scalar(ovf[:], self.ovc[:], wqg[:, 4:5], None, ALU.add)
            VE.inc("dve")
            VE.wait("dve", self.c("dve"))
            VE.tensor_copy(self.ovit[k][:], ovf[:])
            VE.inc("dve")
        self.wqidx_dve = self.c("dve")
        VE.wait("dve", self.c("dve"))
        VE.tensor_scalar(slotm, pos, self.ebase[0:8, 0:1], None, ALU.add)
        VE.inc("dve")
        VE.tensor_scalar(isov, pos, 1025.0, None, ALU.is_ge)
        VE.inc("dve")
        VE.wait("dve", self.c("dve"))
        VE.scalar_tensor_tensor(slotm, isov, self.ebase[0:8, 1:2], slotm,
                                ALU.mult, ALU.add)
        VE.inc("dve")
        VE.tensor_scalar(markt, mfp, -32256.0, 32256.0, ALU.mult, ALU.add)
        VE.inc("dve")
        VE.wait("dve", self.c("dve"))
        VE.tensor_add(slotm, slotm, markt)
        VE.inc("dve")
        for tb in range(self.NCH):
            cs = slice(tb * 128, (tb + 1) * 128)
            TE.wait("dve", self.c("dve"))
            TE.transpose(self.pb[4 + tb % 2][:, 0:8], slotm[0:8, cs], self.identf[0:8, 0:8])
            TE.transpose(self.pt[tb % 2][:, 0:8], self.ew8[0:8, cs], self.ident[0:8, 0:8])
            TE.inc("pe")
            VE.wait("pe", self.c("pe"))
            VE.tensor_copy(slT[:], self.pb[4 + tb % 2][:, 0:8])
            VE.tensor_copy(ewTt[:], self.pt[tb % 2][:, 0:8])
            VE.inc("dve")
            VE.wait("dve", self.c("dve"))
            VE.tensor_reduce(i1T[:, tb:tb + 1], slT[:], AX.X, ALU.min)
            VE.inc("dve")
            VE.wait("dve", self.c("dve"))
            VE.tensor_scalar(eqt[:], slT[:], i1T[:, tb:tb + 1], None, ALU.is_equal)
            VE.inc("dve")
            VE.wait("dve", self.c("dve"))
            VE.scalar_tensor_tensor(t2t[:], eqt[:], 1e6, slT[:], ALU.mult, ALU.add)
            VE.inc("dve")
            VE.wait("dve", self.c("dve"))
            VE.tensor_reduce(i2T[:, tb:tb + 1], t2t[:], AX.X, ALU.min)
            VE.inc("dve")
            VE.wait("dve", self.c("dve"))
            VE.tensor_scalar_min(i2T[:, tb:tb + 1], i2T[:, tb:tb + 1], 24576.0)
            VE.inc("dve")
            VE.tensor_tensor(t2t[:], ewTt[:], eqt[:], ALU.mult)
            VE.inc("dve")
            VE.wait("dve", self.c("dve"))
            VE.reduce_sum(waT[:, tb:tb + 1], t2t[:], axis=AX.X)
            VE.inc("dve")
            VE.reduce_sum(wsumc[:], ewTt[:], axis=AX.X)
            VE.inc("dve")
            VE.wait("dve", self.c("dve"))
            VE.tensor_sub(wbT[:, tb:tb + 1], wsumc[:], waT[:, tb:tb + 1])
            VE.inc("dve")
        sb16a = self.salloc("cp_sb16a", [16, 128], F32)
        sb16b = self.salloc("cp_sb16b", [16, 128], F32)
        TE.wait("dve", self.c("dve"))
        TE.transpose(self.pb[0][0:16, 0:128], i1T[:], self.identf[:])
        TE.transpose(self.pb[1][0:16, 0:128], i2T[:], self.identf[:])
        TE.inc("pe")
        VE.wait("pe", self.c("pe"))
        VE.tensor_copy(sb16a[:], self.pb[0][0:16, 0:128])
        VE.inc("dve")
        VE.wait("dve", self.c("dve"))
        VE.tensor_copy(sb16b[:], self.pb[1][0:16, 0:128])
        VE.inc("dve")
        GP.wait("dve", self.c("dve"))
        GP.dma_start(out=self.idx_dram[0], in_=sb16a[:])
        GP.inc("lb", 16)
        GP.dma_start(out=self.idx_dram[1], in_=sb16b[:])
        GP.inc("lb", 16)
        GP.wait("lb", self.c("lb"))
        for lane in range(2):
            for k in range(8):
                GP.dma_start(out=self.idxt[lane][16 * k:16 * (k + 1), :],
                             in_=self.idx_dram[lane])
                GP.inc("lb", 16)
        idx_ready = self.c("lb")
        idx_pe_free = self.c("pe")
        h2_entry = self.c("dve")
        stm = self.marks("hz")
        for tb in range(self.NCH):
            cs = slice(tb * 128, (tb + 1) * 128)
            par = tb % 2
            TE.wait("dve", self.c("dve"))
            stm.wait(TE, 2)
            for fc in range(NIC):
                TE.transpose(self.pt[par][:, fc * 128:(fc + 1) * 128],
                             self.sB[fc][:, cs], self.ident[:])
            TE.inc("pe")
            VE.wait("pe", self.c("pe"))
            VE.tensor_copy(prestg[par][:, 0:1024], self.pt[par][:, 0:1024])
            VE.inc("dve")
            SY.wait("dve", self.c("dve"))
            SY.dma_start(out=self.h2t_dram[tb * 128:(tb + 1) * 128, :],
                         in_=prestg[par][:])
            SY.inc("hz", 16)
            stm.mark()
        VE.wait("dve", self.c("dve"))
        VE.tensor_copy(wab[:].rearrange("p (a b) -> p a b", b=2)[:, :, 0], waT[:])
        VE.inc("dve")
        VE.wait("dve", self.c("dve"))
        VE.tensor_copy(wab[:].rearrange("p (a b) -> p a b", b=2)[:, :, 1], wbT[:])
        VE.inc("dve")
        SY.wait("dve", self.c("dve"))
        SY.dma_start(out=self.wab_dram.rearrange("(tb p) w -> p tb w", p=128),
                     in_=wab[:].rearrange("p (a b) -> p a b", b=2))
        SY.inc("hz", 16)
        h2t_done = self.c("hz")
        h3 = self.h2t_dram.rearrange("(b r) f -> b r f", b=16)
        for half in range(2):
            SY.wait("hz", h2t_done)
            if half >= 1:
                SY.wait("lb", self.c("lb"))
            for a in range(8):
                SY.dma_start(out=scat[16 * a:16 * (a + 1), :, :],
                             in_=h3[:, a + 64 * half:a + 64 * half + 57:8, :])
                SY.inc("ld", 16)
            GP.wait("ld", self.c("ld"))
            GP.wait("lb", idx_ready)
            GP.wait("hz", self.hz_zero)
            GP.dma_scatter_add(self.hsort, scat[:, :, :],
                               self.idxt[0][:, 64 * half:64 * half + 64],
                               1024, 1024, 1024)
            GP.inc("lb", 16)
            GP.dma_scatter_add(self.hsort, scat[:, :, :],
                               self.idxt[1][:, 64 * half:64 * half + 64],
                               1024, 1024, 1024)
            GP.inc("lb", 16)
        self.scat_done = self.c("lb")
        TE.wait("pe", idx_pe_free)
        self.bar()

    def ph_moe(self):
        T = self.T
        GP, TE, VE, SC, SY = self.GP, self.TE, self.VE, self.SC, self.SY
        wq = [self.salloc(f"mq_wq{i}", [128, WQEL], BF16) for i in range(2)]
        Bh = self.salloc("mq_Bh", [128, 8192], BF16)
        Bacc = self.salloc("mq_Bacc", [128, 8192], BF16)
        hfm = self.salloc("mq_hfm", [128, 8192], BF16)
        outstg = self.salloc("mq_outstg", [128, 2, 1024], BF16)
        ovstg = hfm_ov = None
        b1f = self.salloc("mq_b1f", [128, 8], F32)
        wabG = self.salloc("mq_wabG", [128, 16, 2], F32)
        self.b2sb = self.salloc("mq_b2sb", [8, 1024], BF16)
        GP.dma_start(out=self.b2sb[:], in_=self.dram["b2p"])
        GP.inc("wld", 16)
        b2_wld = self.c("wld")
        hid = self.arena
        ovstg = Bh[:].rearrange("p (a b) -> p a b", b=1024)
        IOA = bass.IndirectOffsetOnAxis
        q_pe = {}
        tile_st = {}
        wq_wld = {}
        m1 = self.marks("act")
        m2 = self.marks("dve")
        mo = self.marks("st")
        def gather_q(q):
            i, g = q // 4, q % 4
            par = q % 2
            ws = "wld0" if par == 0 else "wld1"
            if q == 0:
                GP.wait("dve", self.wqidx_dve)
            if q >= 2:
                GP.wait("pe", q_pe[q - 2])
            r0 = (i * 4 + g) * 128
            if i < 8:
                GP.dma_start(out=wq[par][:, 0:8256],
                             in_=self.dram["wsrc"][r0:r0 + 128, 0:8256])
                GP.inc(ws, 16)
                GP.dma_start(out=wq[par][:, 8256:WQEL],
                             in_=self.dram["wsrc"][r0:r0 + 128, 8256:WQEL])
            else:
                k = i - 8
                GP.indirect_dma_start(
                    out=wq[par][:, 0:8256], out_offset=None,
                    in_=self.dram["wsrc"],
                    in_offset=IOA(self.wqidx[:, g * 3 + k:g * 3 + k + 1], 0),
                    bounds_check=4095, oob_is_err=False)
                GP.inc(ws, 16)
                GP.indirect_dma_start(
                    out=wq[par][:, 8256:WQEL], out_offset=None,
                    in_=self.dram["wsrc"],
                    in_offset=IOA(self.wqidx[:, g * 3 + k:g * 3 + k + 1], 0),
                    element_offset=8256,
                    bounds_check=4095, oob_is_err=False)
            GP.inc(ws, 16)
            wq_wld[q] = (ws, self.c(ws))

        gather_q(0)
        gather_q(1)
        tr_pe = {}
        wf_dves = {}
        for i in range(NTILE):
            SY.wait("lb", self.scat_done)
            if i >= 1:
                SY.wait("pe", tr_pe[i - 1])
            if i >= 9:
                SY.wait("st", tile_st[i - 1])
            if i < 8:
                for cc in range(8):
                    SY.dma_start(
                        out=Bh[:, cc * 1024:(cc + 1) * 1024],
                        in_=self.hsort[i * 2048 + cc * 128:i * 2048 + (cc + 1) * 128, :])
                    SY.inc("ld", 16)
            else:
                GP.wait("lb", self.scat_done)
                GP.wait("pe", tr_pe[i - 1] if i >= 1 else 0)
                GP.wait("st", tile_st[i - 1])
                GP.dma_gather(Bh[:].rearrange("p (a b) -> p a b", b=1024),
                              self.hsort, self.ovit[i - 8][:], 1024, 1024, 1024)
                GP.inc("ld", 16)
            h_ld = self.c("ld")
            for cc in range(8):
                TE.wait("ld", h_ld)
                if cc >= 2:
                    TE.wait("dve", self.c("dve"))
                for fc in range(NIC):
                    TE.transpose(self.pt[cc % 2][:, fc * 128:(fc + 1) * 128],
                                 Bh[:, cc * 1024 + fc * 128:cc * 1024 + (fc + 1) * 128],
                                 self.ident[:])
                TE.inc("pe")
                VE.wait("pe", self.c("pe"))
                VE.tensor_copy(
                    hfm[:].rearrange("p (a b) -> p a b", a=8)[:, :, cc * 128:(cc + 1) * 128],
                    self.pt[cc % 2][:].rearrange("p (a b) -> p a b", a=8))
                VE.inc("dve")
            tr_pe[i] = self.c("pe")
            for g in range(4):
                q = i * 4 + g
                par = q % 2
                TE.wait(*wq_wld[q])
                VE.wait(*wq_wld[q])
                VE.tensor_copy(b1f[:], wq[par][:, 8192:8200])
                VE.inc("dve")
                b1_dve = self.c("dve")
                ph2_prev = self.c("pe")
                for tc2 in range(2):
                    for oc in range(NIC):
                        bank = oc % 2
                        m1.wait(TE, 2)
                        for ic in range(NIC):
                            TE.matmul(self.pb[bank][:, 0:512],
                                      wq[par][:, ic * 1024 + oc * 128:ic * 1024 + (oc + 1) * 128],
                                      hfm[:, ic * 1024 + tc2 * 512:ic * 1024 + (tc2 + 1) * 512],
                                      start=(ic == 0), stop=(ic == NIC - 1))
                        TE.inc("pe")
                        SC.wait("pe", self.c("pe"))
                        SC.wait("dve", b1_dve)
                        if g > 0 and tc2 == 0 and oc < 2:
                            SC.wait("pe", ph2_prev)
                        SC.activation(hid[:, tc2 * 4096 + oc * 512:tc2 * 4096 + (oc + 1) * 512],
                                      self.pb[bank][:, 0:512], AF.Relu,
                                      bias=b1f[:, oc:oc + 1])
                        SC.inc("act")
                        m1.mark()
                    act_hid = self.c("act")
                    for oc2 in range(NIC):
                        bank = 2 + oc2 % 2
                        if oc2 == 0:
                            TE.wait("act", act_hid)
                        m2.wait(TE, 2)
                        for kc in range(NIC):
                            TE.matmul(self.pb[bank][:, 0:512],
                                      wq[par][:, 8200 + kc * 1024 + oc2 * 128:8200 + kc * 1024 + (oc2 + 1) * 128],
                                      hid[:, tc2 * 4096 + kc * 512:tc2 * 4096 + (kc + 1) * 512],
                                      start=(kc == 0), stop=(kc == NIC - 1))
                        TE.inc("pe")
                        VE.wait("pe", self.c("pe"))
                        acc_v = Bacc[:, oc2 * 1024 + tc2 * 512:oc2 * 1024 + (tc2 + 1) * 512]
                        if g == 0:
                            VE.tensor_copy(acc_v, self.pb[bank][:, 0:512])
                        else:
                            VE.wait("dve", self.c("dve"))
                            VE.tensor_tensor(acc_v, acc_v, self.pb[bank][:, 0:512], ALU.add)
                        VE.inc("dve")
                        m2.mark()
                q_pe[q] = self.c("pe")
                if q + 2 < NTILE * 4:
                    gather_q(q + 2)
            acc = Bacc
            for cc in range(8):
                TE.wait("dve", self.c("dve"))
                mo.wait(TE, 2)
                for fc in range(NIC):
                    TE.transpose(self.pt[cc % 2][:, fc * 128:(fc + 1) * 128],
                                 acc[:, fc * 1024 + cc * 128:fc * 1024 + (cc + 1) * 128],
                                 self.ident[:])
                TE.inc("pe")
                VE.wait("pe", self.c("pe"))
                if i < 8:
                    VE.tensor_copy(outstg[:, cc % 2, :], self.pt[cc % 2][:, 0:1024])
                    VE.inc("dve")
                    SY.wait("dve", self.c("dve"))
                    SY.dma_start(
                        out=self.osort[i * 2048 + cc * 128:i * 2048 + (cc + 1) * 128, :],
                        in_=outstg[:, cc % 2, :])
                    SY.inc("st", 16)
                else:
                    VE.tensor_copy(ovstg[:, cc, :], self.pt[cc % 2][:, 0:1024])
                    VE.inc("dve")
                mo.mark()
            if i >= 8:
                GP.wait("dve", self.c("dve"))
                GP.dma_scatter_add(self.osort, ovstg[:, :, :], self.ovit[i - 8][:],
                                   1024, 1024, 1024)
                GP.inc("st", 16)
            tile_st[i] = self.c("st")
        mg1 = wq[0][:, 0:8192].rearrange("p (a b) -> p a b", a=8)
        mg2 = wq[0][:, 8192:16384].rearrange("p (a b) -> p a b", a=8)
        msum = wq[1][:, 0:8192].rearrange("p (a b) -> p a b", a=8)
        mg3 = self.mg_dram.rearrange("(b r) f -> b r f", b=16)
        w3 = self.wab_dram.rearrange("(b r) w -> b r w", b=16)
        for half in range(2):
            GP.wait("st", tile_st[NTILE - 1])
            GP.wait("pe", q_pe[NTILE * 4 - 1])
            if half == 1:
                GP.wait("st", self.c("st"))
            for a in range(8):
                SY.dma_start(out=wabG[16 * a:16 * (a + 1), 0:8, :],
                             in_=w3[:, a + 64 * half:a + 64 * half + 57:8, :])
                SY.inc("ld", 16)
            wab_ld = self.c("ld")
            GP.dma_gather(mg1, self.osort, self.idxt[0][:, 64 * half:64 * half + 64],
                          1024, 1024, 1024)
            GP.inc("lb", 16)
            GP.dma_gather(mg2, self.osort, self.idxt[1][:, 64 * half:64 * half + 64],
                          1024, 1024, 1024)
            GP.inc("lb", 16)
            VE.wait("lb", self.c("lb"))
            VE.wait("ld", wab_ld)
            VE.wait("dve", self.c("dve"))
            for cc in range(8):
                VE.tensor_scalar(msum[:, cc, :], mg1[:, cc, :],
                                 wabG[:, cc, 0:1], None, ALU.mult)
                VE.inc("dve")
                VE.wait("dve", self.c("dve"))
                VE.scalar_tensor_tensor(msum[:, cc, :], mg2[:, cc, :],
                                        wabG[:, cc, 1:2], msum[:, cc, :],
                                        ALU.mult, ALU.add)
                VE.inc("dve")
            VE.wait("dve", self.c("dve"))
            SY.wait("dve", self.c("dve"))
            for a in range(8):
                SY.dma_start(out=mg3[:, a + 64 * half:a + 64 * half + 57:8, :],
                             in_=msum[16 * a:16 * (a + 1), :, :])
                SY.inc("st", 16)
        unwrap_st = self.c("st")
        mm = self.marks("dve")
        for tb in range(self.NCH):
            cs = slice(tb * 128, (tb + 1) * 128)
            SY.wait("st", unwrap_st)
            mm.wait(SY, 2)
            SY.dma_start(out=outstg[:, tb % 2, :], in_=self.mg_dram[tb * 128:(tb + 1) * 128, :])
            SY.inc("ld", 16)
            TE.wait("ld", self.c("ld"))
            mm.wait(TE, 2)
            for fc in range(NIC):
                TE.transpose(self.pt[tb % 2][:, fc * 128:(fc + 1) * 128],
                             outstg[:, tb % 2, fc * 128:(fc + 1) * 128], self.ident[:])
            TE.inc("pe")
            VE.wait("pe", self.c("pe"))
            VE.wait("dve", self.c("dve"))
            for fc in range(NIC):
                VE.tensor_tensor(self.x[fc][:, cs], self.x[fc][:, cs],
                                 self.pt[tb % 2][:, fc * 128:(fc + 1) * 128], ALU.add)
            VE.inc("dve")
            mm.mark()
        mb = self.marks("dve")
        for oc in range(NIC):
            for tc in range(self.NTC):
                ts = slice(tc * 512, (tc + 1) * 512)
                bank = (oc * self.NTC + tc) % 4
                if oc == 0 and tc == 0:
                    TE.wait("dve", self.c("dve"))
                    TE.wait("wld", b2_wld)
                mb.wait(TE, 4)
                TE.matmul(self.pb[bank][:, 0:512], self.b2sb[0:8, oc * 128:(oc + 1) * 128],
                          self.ew8[0:8, ts], start=True, stop=True)
                TE.inc("pe")
                VE.wait("pe", self.c("pe"))
                VE.tensor_tensor(self.x[oc][:, ts], self.x[oc][:, ts],
                                 self.pb[bank][:, 0:512], ALU.add)
                VE.inc("dve")
                mb.mark()
        self.bar()

    def ph_store(self):
        SY = self.SY
        for ic in range(NIC):
            SY.wait("dve", self.c("dve"))
            SY.dma_start(out=self.y_out[ic * 128:(ic + 1) * 128, :], in_=self.x[ic][:])
            SY.inc("st", 16)
        SY.wait("st", self.c("st"))
        self.bar(full=True)

    def emit_all(self, phases):
        ph_map = {
            "consts": self.ph_consts,
            "A": self.ph_precompute_A,
            "l0_attn": self.ph_l0_attn,
            "l0_ffn": self.ph_l0_ffn,
            "store": self.ph_store,
        }
        for ph in phases:
            ph_map[ph]()


def build_nc(T, phases, input_names):
    nc = bass.Bass("TRN2", target_bir_lowering=False)
    with contextlib.ExitStack() as stack:
        mk = MK(nc, stack, T, input_names)
        E = stack.enter_context
        mk.block = E(nc.Block())
        ph_map = {
            "consts": mk.ph_consts,
            "A": mk.ph_precompute_A,
            "l0_attn": mk.ph_l0_attn,
            "l0_ffn": mk.ph_l0_ffn,
            "l1_attn": mk.ph_l1_attn,
            "l1_ffn_attn": mk.ph_l1_ffn_attn,
            "compact": mk.ph_compact,
            "moe": mk.ph_moe,
            "store": mk.ph_store,
        }
        mid = None
        for ph in phases:
            if mid is None and ph not in ("consts", "A"):
                mid = contextlib.ExitStack()
                mk.sA = [mid.enter_context(nc.sbuf_tensor(f"sA{i}", [128, T], BF16)) for i in range(NIC)]
                mk.sB = [mid.enter_context(nc.sbuf_tensor(f"sB{i}", [128, T], BF16)) for i in range(NIC)]
                mk.zrow = mid.enter_context(nc.sbuf_tensor("zrow", [128, 512], BF16))
                mk.rw0 = mid.enter_context(nc.sbuf_tensor("rw0", [2, 1024], F32))
                mk.rw1 = mid.enter_context(nc.sbuf_tensor("rw1", [2, 1024], F32))
                mk.rw2 = mid.enter_context(nc.sbuf_tensor("rw2", [2, 1024], F32))
                mk.crow = mk.rw2[0:1, 512:1024]
                mk.rstd_bf = mid.enter_context(nc.sbuf_tensor("rstd_bf", [2, 1024], BF16))
                mk.msk0 = mid.enter_context(nc.sbuf_tensor("msk0", [128, 128], BF16))
                mk.statef = mid.enter_context(nc.sbuf_tensor("statef", [128, DH], F32))
                mk.otmp = mid.enter_context(nc.sbuf_tensor("otmp", [128, 512], F32))
                mk.tmp2 = mid.enter_context(nc.sbuf_tensor("tmp2", [128, 512], F32))
                midC = contextlib.ExitStack()
                mk.sC = [midC.enter_context(nc.sbuf_tensor(f"sC{i}", [128, T], BF16)) for i in range(3)]
            if ph == "compact" and mid:
                midC.close()
            if ph == "moe" and mid:
                mid.close()
                mk.mid0.close()
                mid = False
            mk.run_phase(ph_map[ph])
        if mid:
            mid.close()
    _mask = {}
    for _lib in _all_libs:
        for _it in _lib.instructions:
            _mask[_it] = _mask.get(_it, 0) | (1 << _lib.index)
    bass_rust.insert_library_loads(nc, _mask, len(_all_libs), _std_lib.index)
    mybir.codegen_inst_isa_subclasses(nc)
    return nc



def dft_consts():
    k = np.arange(D)
    ang = 2.0 * np.pi * np.outer(k, k) / D
    Fr = np.cos(ang).astype(np.float32)
    Fi = (-np.sin(ang)).astype(np.float32)
    Fcr = (Fr / D).astype(np.float32)
    Fci = (-Fi / D).astype(np.float32)
    return Fr, Fi, Fcr, Fci


def pack_lhsT(wT):
    K, M = wT.shape
    r = wT.reshape(K // 128, 128, M // 128, 128).transpose(2, 1, 0, 3)
    return np.ascontiguousarray(r.reshape(M // 128, 128, K).astype(np.float32))


def regroup4(pack):
    return np.ascontiguousarray(
        pack.reshape(2, 4, 128, 1024).transpose(0, 2, 1, 3).reshape(2, 128, 4096))


def pack_cols(v):
    n = v.shape[-1] // 128
    return np.ascontiguousarray(v.reshape(n, 128).T.astype(np.float32))


def host_pack(inputs, T):
    d = {}
    Fr, Fi, Fcr, Fci = dft_consts()
    def rowtiles(M):
        return np.ascontiguousarray(
            M.reshape(8, 128, 1024).transpose(1, 0, 2).reshape(128, 8192).astype(np.float32))
    d["c_FcrP"] = rowtiles(Fcr)
    d["c_FciP"] = rowtiles(Fci)
    d["c_negFciP"] = rowtiles(-Fci)
    frp = pack_lhsT(Fr)
    fip = pack_lhsT(np.ascontiguousarray(-Fi))
    d["c_FrFi"] = np.ascontiguousarray(np.concatenate([frp, fip], axis=2))
    d["c_ident"] = np.eye(128, dtype=np.float32)
    d["c_ones128"] = np.ones((128, 128), dtype=np.float32)
    d["c_onescol"] = np.ones((128, 1), dtype=np.float32)
    d["c_onesT"] = np.ones((1, max(T, 2048)), dtype=np.float32)
    zo = np.zeros((1, 2048), dtype=np.float32)
    zo[0, 512:1024] = 1.0
    zo[0, 1536:2048] = 1.0
    d["c_zo"] = zo
    d["c_maskUT"] = np.triu(np.ones((128, 128), dtype=np.float32))
    hp = np.zeros((128, 2), dtype=np.float32)
    hp[0:64, 0] = 1.0
    hp[64:128, 1] = 1.0
    d["c_hpat"] = hp
    pp = np.zeros((2, 128), dtype=np.float32)
    pp[0, :] = 1.0
    d["c_patplain"] = pp
    wq_ = pack_lhsT(inputs["L0_attn_Wq"].T)
    wk_ = pack_lhsT(inputs["L0_attn_Wk"].T)
    wv_ = pack_lhsT(inputs["L0_attn_Wv"].T)
    d["l0_wqkv"] = np.ascontiguousarray(
        np.concatenate([wv_, wq_, wk_], axis=2))
    d["l0_wo"] = regroup4(pack_lhsT(inputs["L0_attn_Wo"].T))
    wr_ = pack_lhsT(inputs["L0_ffn_w_freq_real"].T)
    wi_ = pack_lhsT(inputs["L0_ffn_w_freq_imag"].T)
    d["l0_wri"] = np.ascontiguousarray(np.concatenate([wr_, wi_], axis=2))

    def hl(g, b):
        m = np.zeros((3, 128), dtype=np.float32)
        m[0, 0:64] = g
        m[1, 64:128] = g
        m[2, 0:64] = b
        m[2, 64:128] = b
        return m
    d["g_hlq"] = hl(np.asarray(inputs["L0_attn_q_gamma"]), np.asarray(inputs["L0_attn_q_beta"]))
    d["g_hlk"] = hl(np.asarray(inputs["L0_attn_k_gamma"]), np.asarray(inputs["L0_attn_k_beta"]))

    def pat(g, b):
        return np.ascontiguousarray(np.stack([np.asarray(g), np.asarray(b)]).astype(np.float32))
    d["g_l1a"] = pat(inputs["L1_attn_ln_gamma"], inputs["L1_attn_ln_beta"])
    d["g_ln1"] = pat(inputs["L1_ffn_ln1_g"], inputs["L1_ffn_ln1_b"])
    d["g_ln2"] = pat(inputs["L1_ffn_ln2_g"], inputs["L1_ffn_ln2_b"])
    bc = np.zeros((128, 240), dtype=np.float32)
    bc[:, 0:8] = pack_cols(np.asarray(inputs["L0_ffn_bias"]))
    for l in range(4):
        bc[:, 8 + l * 8:16 + l * 8] = pack_cols(np.asarray(inputs["L1_attn_fwd_bz"])[l])
        bc[:, 40 + l * 8:48 + l * 8] = pack_cols(np.asarray(inputs["L1_attn_fwd_bh"])[l])
        bc[:, 72 + l * 8:80 + l * 8] = pack_cols(np.asarray(inputs["L1_attn_bwd_bz"])[l])
        bc[:, 104 + l * 8:112 + l * 8] = pack_cols(np.asarray(inputs["L1_attn_bwd_bh"])[l])
    bc[:, 136:144] = pack_cols(np.asarray(inputs["L1_attn_fusion_b"]))
    bc[:, 144:168] = pack_cols(np.asarray(inputs["L1_ffn_b_qkv"]))
    bc[:, 168:176] = pack_cols(np.asarray(inputs["L1_ffn_b_o"]))
    for e in range(8):
        bc[:, 176 + e * 8:184 + e * 8] = pack_cols(np.asarray(inputs["L1_ffn_b2"])[e])
    d["biascols"] = bc
    d["gate_b"] = np.asarray(inputs["L1_ffn_gate_b"]).reshape(8, 1).astype(np.float32)
    for nm, kz, kh in (("fwd_wzh", "L1_attn_fwd_Wz", "L1_attn_fwd_Wh"),
                       ("bwd_wzh", "L1_attn_bwd_Wz", "L1_attn_bwd_Wh")):
        wz = np.asarray(inputs[kz])
        wh = np.asarray(inputs[kh])
        grans = []
        for l in range(4):
            pz = pack_lhsT(wz[l].T)
            ph = pack_lhsT(wh[l].T)
            for g in range(4):
                grans.append(np.concatenate(
                    [pz[2 * g], ph[2 * g], pz[2 * g + 1], ph[2 * g + 1]], axis=1))
        d[nm] = np.ascontiguousarray(np.stack(grans))
    fw = np.asarray(inputs["L1_attn_fusion_W"]).T
    d["fus_top"] = regroup4(pack_lhsT(np.ascontiguousarray(fw[0:1024])))
    d["fus_bot"] = regroup4(pack_lhsT(np.ascontiguousarray(fw[1024:2048])))
    wqkv = pack_lhsT(np.asarray(inputs["L1_ffn_w_qkv"]))
    d["wqkv3"] = np.ascontiguousarray(np.concatenate(
        [wqkv[0:8], wqkv[8:16], wqkv[16:24]], axis=2))
    d["wo2"] = regroup4(pack_lhsT(np.asarray(inputs["L1_ffn_w_o"])))
    d["gate_pack"] = np.ascontiguousarray(
        np.asarray(inputs["L1_ffn_gate_w"]).reshape(8, 128, 8).astype(np.float32))
    w1 = np.asarray(inputs["L1_ffn_W1"]).astype(np.float32)
    w1gb = w1.reshape(8, 8, 128, 4, 1024).transpose(0, 3, 2, 1, 4).reshape(32, 128, 8192)
    w2 = np.asarray(inputs["L1_ffn_W2"]).astype(np.float32)
    w2gb = w2.reshape(8, 4, 8, 128, 1024).transpose(0, 1, 3, 2, 4).reshape(32, 128, 8192)
    b1v = np.asarray(inputs["L1_ffn_b1"]).astype(np.float32)
    wsrc = np.zeros((4096, 16512), dtype=np.float32)
    for e in range(8):
        for g in range(4):
            r0 = (e * 4 + g) * 128
            wsrc[r0:r0 + 128, 0:8192] = w1gb[e * 4 + g]
            wsrc[r0:r0 + 128, 8192:8200] = (
                b1v[e, g * 1024:(g + 1) * 1024].reshape(8, 128).T)
            wsrc[r0:r0 + 128, 8200:16392] = w2gb[e * 4 + g]
    d["wsrc"] = wsrc
    d["b2p"] = np.asarray(inputs["L1_ffn_b2"]).astype(np.float32)
    d["c_identf"] = np.eye(128, dtype=np.float32)
    d["c_irow"] = np.broadcast_to(np.arange(11, dtype=np.float32), (8, 11)).copy()
    pg = (np.arange(128, dtype=np.float32)[:, None]
          + 128.0 * np.arange(4, dtype=np.float32)[None, :])
    d["c_pg"] = np.ascontiguousarray(pg)
    eb = np.zeros((8, 2), dtype=np.float32)
    eb[:, 0] = 2048.0 * np.arange(8) - 1.0
    eb[:, 1] = 15360.0 - 1024.0 * np.arange(8)
    d["c_ebase"] = eb
    ovc = (np.arange(64, dtype=np.float32)[None, :] * 16.0
           + (np.arange(128, dtype=np.float32) % 16.0)[:, None])
    d["c_ovc"] = np.ascontiguousarray(ovc)
    return d


def input_spec(d, T):
    return [(k, v.shape) for k, v in d.items()] + [("xT", (D, T))]


def run_phases(inputs, phases, T=2048, n_cores=NCORE, x_batches=None):
    d = host_pack(inputs, T)
    spec = input_spec(d, T)
    nc = build_nc(T, phases, spec)
    x = np.asarray(inputs["L0_attn_x"], dtype=np.float32)
    in_maps = []
    bsel = x_batches if x_batches is not None else list(range(n_cores))
    for b in bsel:
        m = dict(d)
        m["xT"] = np.ascontiguousarray(x[b, :T, :].T)
        in_maps.append(m)
    res = run_bass_kernel_spmd(nc, in_maps, core_ids=list(range(len(bsel))))
    return np.stack([np.ascontiguousarray(r["y_out"].T) for r in res.results])


def kernel(**inputs):
    phases = ["consts", "A", "l0_attn", "l0_ffn", "l1_attn",
              "l1_ffn_attn", "compact", "moe", "store"]
    return run_phases(inputs, phases)


# version 28
# speedup vs baseline: 1.0308x; 1.0308x over previous
import contextlib
import numpy as np
import concourse.bass as bass
import concourse.mybir as mybir
import bass_rust
from concourse.bass_utils import run_bass_kernel_spmd
from concourse.library_config import all_libraries as _all_libs, standard as _std_lib

F32 = mybir.dt.float32
BF16 = mybir.dt.bfloat16
AF = mybir.ActivationFunctionType
ALU = mybir.AluOpType
AX = mybir.AxisListType

D = 1024
NIC = 8
NHEAD = 16
DH = 64
NPAIR = 8
EPS = 1e-5
NCORE = 8

SEMS = ["ld", "wld", "wld0", "wld1", "wld2", "pe", "dve", "act", "pool", "st", "bar", "bar3",
        "wc", "hz", "lb"]
NTILE = 11
WQEL = 16512


class Proxy:

    def __init__(self, mk, name):
        self._mk = mk
        self._name = name

    @property
    def on(self):
        return self._mk.en == self._name

    def __getattr__(self, attr):
        mk = self._mk
        if self.on:
            real = getattr(mk.eng, attr)

            def call(*a, **k):
                ins = real(*a, **k)
                mk.last = ins
                return ins
            return call

        def noop(*a, **k):
            mk.last = None
            return None
        return noop

    def wait(self, s, n):
        if self.on:
            self._mk.eng.wait_ge(self._mk.sem[s], n)

    def inc(self, s, k=1):
        mk = self._mk
        if self.on:
            assert mk.last is not None, f"inc({s}) with no prior instruction on {self._name}"
            mk.last.then_inc(mk.sem[s], k)
        mk.cnt[s] += k

    def milestone(self, s, wait_sem=None, wait_n=None):
        if self.on:
            if wait_sem is not None:
                self._mk.eng.wait_ge(self._mk.sem[wait_sem], wait_n)
            self._mk.eng.sem_inc(self._mk.sem[s], 1)
        self._mk.cnt[s] += 1


class Marks:

    def __init__(self, mk, sem):
        self.mk = mk
        self.sem = sem
        self.lst = []

    def mark(self):
        self.lst.append(self.mk.c(self.sem))

    def wait(self, proxy, lag):
        if len(self.lst) >= lag:
            proxy.wait(self.sem, self.lst[-lag])


class MK:
    def __init__(self, nc, stack, T, input_names):
        self.nc = nc
        self.T = T
        self.NTC = T // 512
        self.NCH = T // 128
        E = stack.enter_context
        self.E = E
        self.en = None
        self.eng = None
        self.last = None

        self.SY = Proxy(self, "sync")
        self.GP = Proxy(self, "gpsimd")
        self.TE = Proxy(self, "tensor")
        self.VE = Proxy(self, "vector")
        self.SC = Proxy(self, "scalar")

        self.dram = {}
        for name, shape in input_names:
            self.dram[name] = nc.dram_tensor(name, list(shape), F32, kind="ExternalInput").ap()
        self.y_out = nc.dram_tensor("y_out", [D, T], F32, kind="ExternalOutput").ap()
        self.A_dram = nc.dram_tensor("A_scratch", [128, 8192], BF16, kind="Internal").ap()
        I16 = mybir.dt.int16
        I32 = mybir.dt.int32
        self.hsort = nc.dram_tensor("hsort_scratch", [24704, 1024], BF16, kind="Internal").ap()
        self.osort = nc.dram_tensor("osort_scratch", [24704, 1024], BF16, kind="Internal").ap()
        self.h2t_dram = nc.dram_tensor("h2t_scratch", [T, 1024], BF16, kind="Internal").ap()
        self.mg_dram = nc.dram_tensor("mg_scratch", [T, 1024], BF16, kind="Internal").ap()
        self.wab_dram = nc.dram_tensor("wab_scratch", [T, 2], F32, kind="Internal").ap()
        self.idx_dram = nc.dram_tensor("idx_scratch", [2, 16, 128], I16, kind="Internal").ap()

        self.sem = {}
        self.cnt = {}
        for s in SEMS:
            self.sem[s] = E(nc.semaphore(s))
            self.cnt[s] = 0
        self.nbar = 0
        self.nbar3 = 0
        self.agate = {}

        self.pb = [E(nc.psum_tensor(f"pb{i}", [128, 512], F32)) for i in range(6)]
        self.pt = [E(nc.psum_tensor(f"pt{i}", [128, 1024], BF16)) for i in range(2)]
        self.mid0 = contextlib.ExitStack()
        E0 = self.mid0.enter_context

        I16 = mybir.dt.int16
        I32 = mybir.dt.int32
        self.x = [E(nc.sbuf_tensor(f"x{i}", [128, T], F32)) for i in range(NIC)]
        self.arena = E(nc.sbuf_tensor("arena", [128, 8192], BF16))
        self.ones_col = E(nc.sbuf_tensor("ones_col", [128, 1], BF16))
        self.ones_row = E(nc.sbuf_tensor("ones_row", [65, 128], BF16))
        self.ident = E(nc.sbuf_tensor("ident", [128, 128], BF16))
        self.ew8 = E(nc.sbuf_tensor("ew8", [8, T], BF16))
        self.idxt = [E(nc.sbuf_tensor(f"idxt{i}", [128, 128], I16)) for i in range(2)]
        self.wqidx = E(nc.sbuf_tensor("wqidx", [128, 48], I32))
        self.ovit = [E(nc.sbuf_tensor(f"ovit{i}", [128, 64], I16)) for i in range(3)]
        self.ebase = E0(nc.sbuf_tensor("ebase", [8, 2], F32))
        self.ovc = E0(nc.sbuf_tensor("ovc", [128, 64], F32))
        self.maskUT = E0(nc.sbuf_tensor("maskUT", [128, 128], BF16))
        self.hpat = E0(nc.sbuf_tensor("hpat", [128, 2], BF16))
        self.pat_plain = E0(nc.sbuf_tensor("pat_plain", [2, 128], BF16))
        self.pat_l1a = E0(nc.sbuf_tensor("pat_l1a", [2, D], BF16))
        self.pat_ln1 = E0(nc.sbuf_tensor("pat_ln1", [2, D], BF16))
        self.pat_ln2 = E0(nc.sbuf_tensor("pat_ln2", [2, D], BF16))
        self.hl_q = E0(nc.sbuf_tensor("hl_q", [3, 128], BF16))
        self.hl_k = E0(nc.sbuf_tensor("hl_k", [3, 128], BF16))
        self.rsm = E0(nc.sbuf_tensor("rsm", [2, 2048], BF16))
        self.mr3 = E0(nc.sbuf_tensor("mr3", [3, 1024], BF16))
        self.bias = E0(nc.sbuf_tensor("sb_bias", [128, 240], F32))
        self.gateb = E0(nc.sbuf_tensor("sb_gateb", [8, 1], F32))
        self.sq0 = E0(nc.sbuf_tensor("sq0", [128, 512], BF16))
        self.sq1 = E0(nc.sbuf_tensor("sq1", [128, 512], BF16))
        self.ccol = E0(nc.sbuf_tensor("ccol", [128, 8], F32))
        self.gatew = E0(nc.sbuf_tensor("gatew", [128, 64], BF16))
        self.c1b = E0(nc.sbuf_tensor("c1b", [1, 8], BF16))
        self.irow = E0(nc.sbuf_tensor("irow", [8, 16], F32))
        self.pgt = E0(nc.sbuf_tensor("pgt", [128, 4], F32))

        self.bslot = {"l0ffn": 0, "fwd_bz": 8, "fwd_bh": 40, "bwd_bz": 72,
                      "bwd_bh": 104, "fus": 136, "bqkv": 144, "bo": 168, "b2": 176}

    def aslot(self, off, ln):
        assert off + ln <= 8192
        return self.arena[:, off:off + ln]

    def wload(self, key, out_ap, src_ap, ws):
        g = None
        if isinstance(key, tuple) and isinstance(key[0], int):
            o, ln_ = key
            for kk, v in list(self.agate.items()):
                if (isinstance(kk, tuple) and len(kk) == 2 and isinstance(kk[0], int)
                        and kk[0] < o + ln_ and o < kk[0] + kk[1]):
                    g = v if g is None else max(g, v)
        else:
            g = self.agate.get(key)
        if g is not None:
            self.GP.wait("pe", g)
        self.GP.dma_start(out=out_ap, in_=src_ap)
        self.GP.inc(ws, 16)

    def wdone(self, *keys):
        for k in keys:
            if isinstance(k, tuple) and isinstance(k[0], int):
                o, ln_ = k
                for (ko, kl) in [kk for kk in self.agate
                                 if isinstance(kk, tuple) and isinstance(kk[0], int)]:
                    if ko >= o and ko + kl <= o + ln_:
                        del self.agate[(ko, kl)]
            self.agate[k] = self.c("pe")

    def marks(self, sem):
        return Marks(self, sem)

    def bar(self, full=False):
        if full:
            self.nbar += 1
            self.eng.sem_inc(self.sem["bar"], 1)
            self.eng.wait_ge(self.sem["bar"], 5 * self.nbar)
        else:
            self.nbar3 += 1
            if self.en in ("tensor", "vector", "scalar"):
                self.eng.sem_inc(self.sem["bar3"], 1)
                self.eng.wait_ge(self.sem["bar3"], 3 * self.nbar3)

    def c(self, s):
        return self.cnt[s]

    def salloc(self, name, shape, dtype):
        if name not in self._scoped:
            assert self._first_pass, f"salloc({name}) introduced after first pass"
            self._scoped[name] = self._phase_stack.enter_context(
                self.nc.sbuf_tensor(name, list(shape), dtype))
        return self._scoped[name]

    def run_phase(self, fn):
        cnt0 = dict(self.cnt)
        nbar0 = self.nbar
        nbar30 = self.nbar3
        agate0 = dict(self.agate)
        final = None
        self._phase_stack = contextlib.ExitStack()
        self._scoped = {}
        regs = (("sync", self.block.sync), ("gpsimd", self.block.gpsimd),
                ("tensor", self.block.tensor), ("vector", self.block.vector),
                ("scalar", self.block.scalar))
        for en, reg in regs:
            self.cnt = dict(cnt0)
            self.nbar = nbar0
            self.nbar3 = nbar30
            self.agate = dict(agate0)
            self.en = en
            self._first_pass = final is None

            def _run(e, _fn=fn):
                self.eng = e
                _fn()
            reg(_run)
            if final is None:
                final = (dict(self.cnt), self.nbar, dict(self.agate), self.nbar3)
            else:
                assert (self.cnt, self.nbar, self.agate, self.nbar3) == \
                    (final[0], final[1], final[2], final[3]), \
                    f"{en}: counter divergence {self.cnt} vs {final[0]}"
        self._phase_stack.close()
        self.cnt = dict(final[0])
        self.nbar = final[1]
        self.agate = dict(final[2])
        self.nbar3 = final[3]

    def ph_consts(self):
        SY, GP = self.SY, self.GP
        T = self.T
        for ic in range(NIC):
            SY.dma_start(out=self.x[ic][:], in_=self.dram["xT"][ic * 128:(ic + 1) * 128, :])
            SY.inc("ld", 16)
        SY.dma_start(out=self.bias[:], in_=self.dram["biascols"])
        SY.inc("ld", 16)
        SY.dma_start(out=self.gateb[:], in_=self.dram["gate_b"])
        SY.inc("ld", 16)
        for dst, src in [
            (self.ones_col[:], self.dram["c_onescol"]),
            (self.ones_row[:], self.dram["c_ones128"][0:65, :]),
            (self.ident[:], self.dram["c_ident"]),
            (self.maskUT[:], self.dram["c_maskUT"]),
            (self.hpat[:], self.dram["c_hpat"]),
            (self.pat_plain[:], self.dram["c_patplain"]),
            (self.pat_l1a[:], self.dram["g_l1a"]),
            (self.pat_ln1[:], self.dram["g_ln1"]),
            (self.pat_ln2[:], self.dram["g_ln2"]),
            (self.hl_q[:], self.dram["g_hlq"]),
            (self.hl_k[:], self.dram["g_hlk"]),
            (self.rsm[1:2, :], self.dram["c_zo"]),
            (self.mr3[2:3, :], self.dram["c_onesT"][0:1, 0:1024]),
            (self.irow[0:8, 0:11], self.dram["c_irow"]),
            (self.pgt[:], self.dram["c_pg"]),
            (self.ebase[:], self.dram["c_ebase"]),
            (self.ovc[:], self.dram["c_ovc"]),
        ]:
            GP.dma_start(out=dst, in_=src)
            GP.inc("wld", 16)
        SY.wait("ld", self.c("ld"))
        GP.wait("wld", self.c("wld"))
        self.bar(full=True)


    def ln(self, src, out, pat, save=False, src_f32=True, fillers=None):
        T, NTC = self.T, self.NTC
        SY, GP, TE, VE, SC = self.SY, self.GP, self.TE, self.VE, self.SC
        c1 = 1.0 / D
        sb = out if src_f32 else src
        if src_f32:
            for ic in range(NIC):
                SC.wait("dve", self.c("dve"))
                SC.copy(out[ic][:], src[ic][:])
                SC.inc("act")
        entry_dve = self.c("dve")
        cast_done = self.c("act")
        pe_stats = {}
        rows_read = {}
        rows_done = {}
        lnexp_done = {}
        bc_done = {}
        lm = self.marks("dve")

        def emit_stats(tc):
            ts = slice(tc * 512, (tc + 1) * 512)
            pe_base = self.c("pe")
            for ic in range(NIC):
                sq = self.sq0 if ic % 2 == 0 else self.sq1
                SC.wait("act", cast_done)
                if ic == 0 and tc == 0:
                    SC.wait("dve", entry_dve)
                if ic >= 2:
                    SC.wait("pe", pe_base + ic - 1)
                elif tc > 0:
                    SC.wait("pe", pe_base)
                SC.activation(sq[:, 0:512], sb[ic][:, ts], AF.Square)
                SC.inc("act")
                TE.wait("act", self.c("act"))
                if ic == 0 and tc > 1:
                    TE.wait("dve", rows_read[tc - 2])
                if ic == 0 and tc <= 1:
                    TE.wait("dve", entry_dve)
                sbank = self.pb[4 + tc % 2]
                TE.matmul(sbank[0:1, 0:512], self.ones_col[:], sb[ic][:, ts],
                          start=(ic == 0), stop=(ic == NIC - 1))
                TE.matmul(sbank[32:33, 0:512], self.ones_col[:], sq[:, 0:512],
                          start=(ic == 0), stop=(ic == NIC - 1))
                TE.inc("pe")
            pe_stats[tc] = self.c("pe")

        def emit_rows(tc):
            par = (tc % 2) * 512
            ps = slice(par, par + 512)
            VE.wait("pe", pe_stats[tc])
            if tc >= 2:
                VE.wait("act", lnexp_done[tc - 2])
                VE.wait("pe", bc_done.get(tc - 2, 0))
            sbank = self.pb[4 + tc % 2]
            VE.tensor_scalar_mul(self.rw0[0:1, ps], sbank[0:1, 0:512], c1)
            VE.tensor_scalar_mul(self.rw1[0:1, ps], sbank[32:33, 0:512], c1)
            VE.inc("dve")
            rows_read[tc] = self.c("dve")
            VE.wait("dve", self.c("dve"))
            VE.tensor_mul(self.rw2[0:1, ps], self.rw0[0:1, ps], self.rw0[0:1, ps])
            VE.inc("dve")
            VE.wait("dve", self.c("dve"))
            VE.scalar_tensor_tensor(self.rw1[0:1, ps], self.rw1[0:1, ps], EPS,
                                    self.rw2[0:1, ps], ALU.add, ALU.subtract)
            VE.inc("dve")
            SC.wait("dve", self.c("dve"))
            SC.activation(self.rw1[0:1, ps], self.rw1[0:1, ps], AF.Ln)
            SC.inc("act")
            SC.wait("act", self.c("act"))
            SC.activation(self.rw2[0:1, ps], self.rw1[0:1, ps], AF.Exp, scale=-0.5)
            SC.inc("act")
            lnexp_done[tc] = self.c("act")
            VE.wait("act", self.c("act"))
            VE.scalar_tensor_tensor(self.rsm[0:1, (tc % 2) * 1024 + 512:(tc % 2) * 1024 + 1024],
                                    self.rw0[0:1, ps], -1.0,
                                    self.rw2[0:1, ps], ALU.mult, ALU.mult)
            VE.inc("dve")
            VE.tensor_copy(self.rsm[0:1, (tc % 2) * 1024:(tc % 2) * 1024 + 512],
                           self.rw2[0:1, ps])
            VE.inc("dve")
            rows_done[tc] = self.c("dve")

        def emit_apply(tc):
            ts = slice(tc * 512, (tc + 1) * 512)
            rbase = (tc % 2) * 1024
            for ic in range(NIC):
                bi = ic % 2
                if ic < 2:
                    TE.wait("dve", rows_done[tc])
                lm.wait(TE, 2)
                po = (ic * 128) % pat.shape[1]
                TE.matmul(self.pb[bi][:, 0:512], pat[0:1, po:po + 128],
                          self.rsm[0:1, rbase:rbase + 512], start=True, stop=True)
                TE.matmul(self.pb[2 + bi][:, 0:512], pat[0:2, po:po + 128],
                          self.rsm[0:2, rbase + 512:rbase + 1024], start=True, stop=True)
                TE.inc("pe")
                VE.wait("pe", self.c("pe"))
                VE.tensor_tensor(out[ic][:, ts], src[ic][:, ts], self.pb[bi][:, 0:512], ALU.mult)
                VE.inc("dve")
                VE.wait("dve", self.c("dve"))
                VE.tensor_tensor(out[ic][:, ts], out[ic][:, ts], self.pb[2 + bi][:, 0:512], ALU.add)
                VE.inc("dve")
                lm.mark()
            bc_done[tc] = self.c("pe")

        emit_stats(0)
        for tc in range(NTC):
            if tc + 1 < NTC:
                emit_stats(tc + 1)
            emit_rows(tc)
            if tc >= 1:
                emit_apply(tc - 1)
                if fillers:
                    fillers(tc - 1)
        emit_apply(NTC - 1)
        if fillers:
            fillers(NTC - 1)

    def proj(self, gname, rhs_tiles, nk, evac):
        T, NTC = self.T, self.NTC
        GP, TE = self.GP, self.TE
        mkv = self.marks("dve")
        for gr in range(2):
            w_ = self.aslot(4096 * (gr % 2), 4096)
            ws = "wld0" if gr % 2 == 0 else "wld1"
            self.wload((4096 * (gr % 2), 4096), w_, self.dram[gname][gr], ws)
            wld_here = (ws, self.c(ws))
            for oc4 in range(4):
                oc = gr * 4 + oc4
                for tc in range(NTC):
                    ts = slice(tc * 512, (tc + 1) * 512)
                    bank = (oc * NTC + tc) % 4
                    if oc4 == 0 and tc == 0:
                        TE.wait(*wld_here)
                        if gr == 0:
                            TE.wait("dve", self.c("dve"))
                    mkv.wait(TE, 4)
                    for ic in range(nk):
                        TE.matmul(self.pb[bank][:, 0:512],
                                  w_[:, oc4 * 1024 + ic * 128:oc4 * 1024 + (ic + 1) * 128],
                                  rhs_tiles[ic][:, ts], start=(ic == 0), stop=(ic == nk - 1))
                    TE.inc("pe")
                    evac(oc, tc, ts, bank)
                    mkv.mark()
            self.wdone((4096 * (gr % 2), 4096))

    def ph_l0_attn(self):
        T, NTC, NCH = self.T, self.NTC, self.NCH
        SY, GP, TE, VE, SC = self.SY, self.GP, self.TE, self.VE, self.SC
        self.ktr = self.salloc("ktr", [128, T], BF16)
        self.vtr = self.salloc("vtr", [128, T], BF16)
        self.spref = self.salloc("spref", [128, 16 * DH], BF16)
        self.ln(self.x, self.sA, self.pat_plain, src_f32=True)
        for pair in range(NPAIR):
            pb_ = pair % 2
            blk = self.aslot(3072 * pb_, 3072)
            wv, wq, wk = (self.aslot(3072 * pb_, 1024), self.aslot(3072 * pb_ + 1024, 1024),
                          self.aslot(3072 * pb_ + 2048, 1024))
            qt, kt, vt = self.sC[0], self.sC[1], self.sC[2]
            ws = "wld0" if pair % 2 == 0 else "wld1"
            if pair >= 1:
                GP.wait("dve", self.c("dve"))
            self.wload((3072 * pb_, 3072), blk, self.dram["l0_wqkv"][pair], ws)
            wld_here = (ws, self.c(ws))
            pm = self.marks("dve")
            for which, (wt, dst) in enumerate([(wv, vt), (wq, qt), (wk, kt)]):
                for tc in range(NTC):
                    ts = slice(tc * 512, (tc + 1) * 512)
                    bank = (which * NTC + tc) % 4
                    if which == 0 and tc == 0:
                        TE.wait(*wld_here)
                        TE.wait("dve", self.c("dve"))
                    pm.wait(TE, 4)
                    for ic in range(NIC):
                        TE.matmul(self.pb[bank][:, 0:512], wt[:, ic * 128:(ic + 1) * 128],
                                  self.sA[ic][:, ts], start=(ic == 0), stop=(ic == NIC - 1))
                    TE.inc("pe")
                    VE.wait("pe", self.c("pe"))
                    VE.tensor_copy(dst[:, ts], self.pb[bank][:, 0:512])
                    VE.inc("dve")
                    pm.mark()
            self.wdone((3072 * pb_, 3072))
            vfill_marks = self.marks("dve")
            vfill_entry = self.c("dve")

            def mk_vfill(ch, _vt=vt):
                def go():
                    TE, VE = self.TE, self.VE
                    cs = slice(ch * 128, (ch + 1) * 128)
                    tb = ch % 2
                    if ch < 2:
                        TE.wait("dve", vfill_entry)
                    vfill_marks.wait(TE, 2)
                    TE.transpose(self.pt[tb][:, 128:256], _vt[:, cs], self.ident[:])
                    TE.inc("pe")
                    VE.wait("pe", self.c("pe"))
                    VE.tensor_copy(self.vtr[:, cs], self.pt[tb][:, 128:256])
                    VE.inc("dve")
                    vfill_marks.mark()
                return go
            self.head_ln2(qt, kt, [mk_vfill(ch) for ch in range(self.NCH)])
            self.l0_linattn(qt, kt, vt, pair)
        bslot = None
        def evac(oc, tc, ts, bank):
            VE = self.VE
            VE.wait("pe", self.c("pe"))
            VE.tensor_add(self.x[oc][:, ts], self.x[oc][:, ts], self.pb[bank][:, 0:512])
            VE.inc("dve")
        self.proj("l0_wo", self.sB, NIC, evac)

    def head_ln2(self, qt, kt, fillers=None):
        T, NTC = self.T, self.NTC
        TE, VE, SC = self.TE, self.VE, self.SC
        c1 = 1.0 / DH
        qs = slice(0, 512)
        ks = slice(512, 1024)
        pe_stats = {}
        rows_read = {}
        rows_done = {}
        lnexp_done = {}
        bc_done = {}
        apply_dve = {}
        entry_dve = self.c("dve")

        def emit_stats(tc):
            ts = slice(tc * 512, (tc + 1) * 512)
            qb = self.pb[2 + tc % 2]
            kb = self.pb[4 + tc % 2]
            SC.wait("dve", apply_dve.get(tc - 2, entry_dve))
            if tc >= 1:
                SC.wait("pe", pe_stats[tc - 1])
            SC.activation(self.sq0[:, 0:512], qt[:, ts], AF.Square)
            SC.activation(self.sq1[:, 0:512], kt[:, ts], AF.Square)
            SC.inc("act")
            TE.wait("act", self.c("act"))
            if tc > 1:
                TE.wait("dve", rows_read[tc - 2])
            if tc <= 1:
                TE.wait("dve", entry_dve)
            TE.matmul(qb[0:2, 0:512], self.hpat[:], qt[:, ts], start=True, stop=True)
            TE.matmul(qb[32:34, 0:512], self.hpat[:], self.sq0[:, 0:512], start=True, stop=True)
            TE.matmul(kb[0:2, 0:512], self.hpat[:], kt[:, ts], start=True, stop=True)
            TE.matmul(kb[32:34, 0:512], self.hpat[:], self.sq1[:, 0:512], start=True, stop=True)
            TE.inc("pe")
            pe_stats[tc] = self.c("pe")

        def emit_rows(tc):
            qb = self.pb[2 + tc % 2]
            kb = self.pb[4 + tc % 2]
            VE.wait("pe", pe_stats[tc])
            if tc >= 1:
                VE.wait("act", lnexp_done[tc - 1])
                VE.wait("pe", bc_done[tc - 1])
            VE.tensor_scalar_mul(self.rw0[0:2, qs], qb[0:2, 0:512], c1)
            VE.tensor_scalar_mul(self.rw1[0:2, qs], qb[32:34, 0:512], c1)
            VE.tensor_scalar_mul(self.rw0[0:2, ks], kb[0:2, 0:512], c1)
            VE.tensor_scalar_mul(self.rw1[0:2, ks], kb[32:34, 0:512], c1)
            VE.inc("dve")
            rows_read[tc] = self.c("dve")
            VE.wait("dve", self.c("dve"))
            VE.tensor_mul(self.rw2[0:2, :], self.rw0[0:2, :], self.rw0[0:2, :])
            VE.inc("dve")
            VE.wait("dve", self.c("dve"))
            VE.scalar_tensor_tensor(self.rw1[0:2, :], self.rw1[0:2, :], EPS,
                                    self.rw2[0:2, :], ALU.add, ALU.subtract)
            VE.inc("dve")
            SC.wait("dve", self.c("dve"))
            SC.activation(self.rw1[0:2, :], self.rw1[0:2, :], AF.Ln)
            SC.inc("act")
            SC.wait("act", self.c("act"))
            SC.activation(self.rw2[0:2, :], self.rw1[0:2, :], AF.Exp, scale=-0.5)
            SC.inc("act")
            lnexp_done[tc] = self.c("act")
            VE.wait("act", self.c("act"))
            VE.scalar_tensor_tensor(self.mr3[0:2, :], self.rw0[0:2, :], -1.0,
                                    self.rw2[0:2, :], ALU.mult, ALU.mult)
            VE.inc("dve")
            VE.tensor_copy(self.rstd_bf[0:2, :], self.rw2[0:2, :])
            VE.inc("dve")
            rows_done[tc] = self.c("dve")

        def emit_apply(tc):
            ts = slice(tc * 512, (tc + 1) * 512)
            for hl_pat, t_, cols in ((self.hl_q, qt, qs), (self.hl_k, kt, ks)):
                TE.wait("dve", rows_done[tc] if t_ is qt else self.c("dve"))
                TE.matmul(self.pb[0][:, 0:512], hl_pat[0:2, :], self.rstd_bf[0:2, cols],
                          start=True, stop=True)
                TE.matmul(self.pb[1][:, 0:512], hl_pat[0:3, :], self.mr3[0:3, cols],
                          start=True, stop=True)
                TE.inc("pe")
                VE.wait("pe", self.c("pe"))
                VE.tensor_tensor(self.sq0[:, 0:512] if t_ is qt else self.sq1[:, 0:512],
                                 t_[:, ts], self.pb[0][:, 0:512], ALU.mult)
                VE.inc("dve")
                VE.wait("dve", self.c("dve"))
                VE.tensor_tensor(t_[:, ts],
                                 self.sq0[:, 0:512] if t_ is qt else self.sq1[:, 0:512],
                                 self.pb[1][:, 0:512], ALU.add)
                VE.inc("dve")
            bc_done[tc] = self.c("pe")
            apply_dve[tc] = self.c("dve")

        fillers = list(fillers or [])

        def fill(n):
            for _ in range(n):
                if fillers:
                    fillers.pop(0)()

        emit_stats(0)
        if NTC > 1:
            emit_stats(1)
        emit_rows(0)
        fill(4)
        for tc in range(1, NTC):
            emit_apply(tc - 1)
            if tc + 1 < NTC:
                emit_stats(tc + 1)
            emit_rows(tc)
            fill(4)
        emit_apply(NTC - 1)
        fill(len(fillers))

    def l0_linattn(self, qt, kt, vt, pair):
        NCH = self.NCH
        TE, VE, SC = self.TE, self.VE, self.SC
        ob = self.sB[pair]
        ktr = self.ktr
        vtr = self.vtr
        tm = self.marks("dve")
        entry = self.c("dve")

        def emit_transposes(ch_lo, ch_hi):
            for ch in range(ch_lo, ch_hi):
                cs = slice(ch * 128, (ch + 1) * 128)
                tb = ch % 2
                if ch < 2:
                    TE.wait("dve", entry)
                tm.wait(TE, 2)
                TE.transpose(self.pt[tb][:, 0:128], kt[:, cs], self.ident[:])
                TE.inc("pe")
                VE.wait("pe", self.c("pe"))
                VE.tensor_copy(ktr[:, cs], self.pt[tb][:, 0:128])
                VE.inc("dve")
                tm.mark()

        def emit_kv_round(rnd, nch8):
            TE.wait("dve", self.c("dve"))
            for ch8 in range(nch8):
                ch = rnd * 8 + ch8
                for h in range(2):
                    TE.matmul(self.pb[4 + h][0:64, ch8 * 64:(ch8 + 1) * 64],
                              ktr[:, ch * 128 + 64 * h:ch * 128 + 64 * h + 64],
                              vtr[:, ch * 128 + 64 * h:ch * 128 + 64 * h + 64],
                              start=True, stop=True)
            TE.inc("pe")
            VE.wait("pe", self.c("pe"))
            for ch8 in range(nch8):
                ch = rnd * 8 + ch8
                for h in range(2):
                    hs = slice(64 * h, 64 * h + 64)
                    VE.tensor_add(self.statef[hs, :], self.statef[hs, :],
                                  self.pb[4 + h][0:64, ch8 * 64:(ch8 + 1) * 64])
                    VE.inc("dve")
                    VE.wait("dve", self.c("dve"))
                    VE.tensor_copy(self.spref[hs, ch * 64:(ch + 1) * 64], self.statef[hs, :])
                    VE.inc("dve")

        VE.wait("pe", self.c("pe"))
        VE.memset(self.statef[:], 0.0)
        VE.inc("dve")
        if NCH > 8:
            emit_transposes(0, 8)
            emit_kv_round(0, 8)
            emit_transposes(8, NCH)
            emit_kv_round(1, NCH - 8)
        else:
            emit_transposes(0, NCH)
            emit_kv_round(0, NCH)
        spref_done = self.c("dve")
        steps = [(ch, h) for ch in range(NCH) for h in range(2)]
        ns = len(steps)
        s_pe = {}
        msk_dve = {}
        o_pe = {}
        evac_dve = {}
        evac_act = {}
        mskts = [self.msk0[:, 0:128], self.sq1[:, 0:128], self.sq0[:, 0:128], self.sq1[:, 128:256]]

        def emit_s(i):
            ch, h = steps[i]
            cs = slice(ch * 128, (ch + 1) * 128)
            hs = slice(64 * h, 64 * h + 64)
            if i >= 2:
                TE.wait("dve", msk_dve[i - 2])
            TE.matmul(self.pb[i % 2][:, 0:128], kt[hs, cs], qt[hs, cs], start=True, stop=True)
            TE.inc("pe")
            s_pe[i] = self.c("pe")

        def emit_mask(i):
            VE.wait("pe", s_pe[i])
            if i >= 4:
                VE.wait("pe", o_pe[i - 4])
            VE.tensor_tensor(mskts[i % 4], self.pb[i % 2][:, 0:128], self.maskUT[:], ALU.mult)
            VE.inc("dve")
            msk_dve[i] = self.c("dve")

        def emit_o(i):
            ch, h = steps[i]
            cs = slice(ch * 128, (ch + 1) * 128)
            hs = slice(64 * h, 64 * h + 64)
            if i >= 4:
                TE.wait("dve", evac_dve[i - 4])
            TE.wait("dve", msk_dve[i])
            bank = self.pb[2 + i % 4]
            TE.matmul(bank[0:64, 0:128],
                      vtr[:, ch * 128 + 64 * h:ch * 128 + 64 * h + 64],
                      mskts[i % 4], start=True, stop=(ch == 0))
            if ch > 0:
                TE.matmul(bank[0:64, 0:128],
                          self.spref[hs, (ch - 1) * 64:ch * 64],
                          qt[hs, cs], start=False, stop=True)
            TE.inc("pe")
            o_pe[i] = self.c("pe")

        def emit_evac(i):
            ch, h = steps[i]
            cs = slice(ch * 128, (ch + 1) * 128)
            hs = slice(64 * h, 64 * h + 64)
            VE.wait("pe", o_pe[i])
            VE.tensor_copy(ob[hs, cs], self.pb[2 + i % 4][0:64, 0:128])
            VE.inc("dve")
            evac_dve[i] = self.c("dve")
            evac_act[i] = self.c("act")

        TE.wait("dve", spref_done)
        emit_s(0)
        emit_s(1)
        emit_mask(0)
        for i in range(ns - 2):
            emit_o(i)
            emit_s(i + 2)
            emit_mask(i + 1)
            emit_evac(i)
        emit_o(ns - 2)
        emit_mask(ns - 1)
        emit_evac(ns - 2)
        emit_o(ns - 1)
        emit_evac(ns - 1)

    def ph_precompute_A(self):
        nc = self.nc
        SY, GP, TE, VE = self.SY, self.GP, self.TE, self.VE
        if True:
            fc0t = self.salloc("fc0t", [128, 8192], BF16)
            fc1t = self.salloc("fc1t", [128, 8192], BF16)
            fc0 = [fc0t[:, i * 1024:(i + 1) * 1024] for i in range(NIC)]
            fc1 = [fc1t[:, i * 1024:(i + 1) * 1024] for i in range(NIC)]
            pq = [self.salloc(f"pq_{i}", [128, 2048], BF16) for i in range(NIC)]
            for pas in range(2):
                if pas == 0:
                    GP.dma_start(out=fc0t[:], in_=self.dram["c_FcrP"])
                    GP.inc("wld", 16)
                    GP.dma_start(out=fc1t[:], in_=self.dram["c_negFciP"])
                    GP.inc("wld", 16)
                else:
                    GP.wait("pe", self.c("pe"))
                    GP.dma_start(out=fc1t[:], in_=self.dram["c_FciP"])
                    GP.inc("wld", 16)
                GP.milestone("pool", "wld", self.c("wld"))
                fc_ready = self.c("pool")
                dsto = 0 if pas == 0 else 1024
                w0, w1 = (fc0, fc1) if pas == 0 else (fc1, fc0)
                for oc in range(NIC):
                    pb_ = oc % 2
                    wr_, wi_ = self.aslot(4096 * pb_, 1024), self.aslot(4096 * pb_ + 1024, 1024)
                    ws = "wld0" if oc % 2 == 0 else "wld1"
                    self.wload((4096 * pb_, 2048), self.aslot(4096 * pb_, 2048),
                               self.dram["l0_wri"][oc], ws)
                    wld_here = (ws, self.c(ws))
                    for jt in range(2):
                        js = slice(jt * 512, (jt + 1) * 512)
                        bank = jt
                        TE.wait("pool", fc_ready)
                        TE.wait(*wld_here)
                        TE.wait("dve", self.c("dve"))
                        for ic in range(NIC):
                            TE.matmul(self.pb[bank][:, 0:512], wr_[:, ic * 128:(ic + 1) * 128],
                                      (fc0 if pas == 0 else fc1)[ic][:, js],
                                      start=(ic == 0), stop=False)
                        for ic in range(NIC):
                            TE.matmul(self.pb[bank][:, 0:512], wi_[:, ic * 128:(ic + 1) * 128],
                                      (fc1 if pas == 0 else fc0)[ic][:, js],
                                      start=False, stop=(ic == NIC - 1))
                        TE.inc("pe")
                        VE.wait("pe", self.c("pe"))
                        VE.tensor_copy(pq[oc][:, dsto + jt * 512:dsto + (jt + 1) * 512],
                                       self.pb[bank][:, 0:512])
                        VE.inc("dve")
                    self.wdone((4096 * pb_, 2048))
            stage1_dve = self.c("dve")
            for oc in range(NIC):
                pb_ = oc % 2
                fr_, fi_ = self.aslot(4096 * pb_, 1024), self.aslot(4096 * pb_ + 1024, 1024)
                ws = "wld0" if oc % 2 == 0 else "wld1"
                self.wload((4096 * pb_, 2048), self.aslot(4096 * pb_, 2048),
                           self.dram["c_FrFi"][oc], ws)
                wld_here = (ws, self.c(ws))
                for jt in range(2):
                    js = slice(jt * 512, (jt + 1) * 512)
                    bank = jt
                    TE.wait(*wld_here)
                    if oc == 0 and jt == 0:
                        TE.wait("dve", stage1_dve)
                    TE.wait("pool", max(0, self.c("pool") - 1))
                    for ic in range(NIC):
                        TE.matmul(self.pb[bank][:, 0:512], fr_[:, ic * 128:(ic + 1) * 128],
                                  pq[ic][:, js], start=(ic == 0), stop=False)
                    for ic in range(NIC):
                        TE.matmul(self.pb[bank][:, 0:512], fi_[:, ic * 128:(ic + 1) * 128],
                                  pq[ic][:, 1024 + jt * 512:1024 + (jt + 1) * 512],
                                  start=False, stop=(ic == NIC - 1))
                    TE.inc("pe")
                    VE.wait("pe", self.c("pe"))
                    VE.tensor_copy((self.sq0 if jt == 0 else self.sq1)[:, 0:512],
                                   self.pb[bank][:, 0:512])
                    VE.inc("dve")
                    SY.wait("dve", self.c("dve"))
                    SY.dma_start(out=self.A_dram[:, oc * 1024 + jt * 512:oc * 1024 + (jt + 1) * 512],
                                 in_=(self.sq0 if jt == 0 else self.sq1)[:, 0:512])
                    SY.inc("st", 16)
                    SY.milestone("pool", "st", self.c("st"))
                self.wdone((4096 * pb_, 2048))
            TE.wait("pool", self.c("pool"))
            VE.wait("pool", self.c("pool"))
            self.SC.wait("pool", self.c("pool"))

    def ph_l0_ffn(self):
        T, NTC = self.T, self.NTC
        GP, TE, VE = self.GP, self.TE, self.VE
        SY = self.SY
        VE.wait("dve", self.c("dve"))
        VE.memset(self.zrow[:], 0.0)
        VE.inc("dve")
        SY.wait("dve", self.c("dve"))
        for e in range(8):
            for r in range(8):
                for hh in range(2):
                    SY.dma_start(
                        out=self.hsort[e * 2048 + r * 128:e * 2048 + (r + 1) * 128,
                                       hh * 512:(hh + 1) * 512],
                        in_=self.zrow[:])
                    SY.inc("hz", 16)
        for r in range(128, 192):
            for hh in range(2):
                SY.dma_start(out=self.hsort[r * 128:(r + 1) * 128, hh * 512:(hh + 1) * 512],
                             in_=self.zrow[:])
                SY.inc("hz", 16)
        for r in range(128, 193):
            for hh in range(2):
                SY.dma_start(out=self.osort[r * 128:(r + 1) * 128, hh * 512:(hh + 1) * 512],
                             in_=self.zrow[:])
                SY.inc("hz", 16)
        self.hz_zero = self.c("hz")
        self.ln(self.x, self.sA, self.pat_plain, src_f32=True)
        GP.wait("pool", self.c("pool"))
        self.wload((0, 8192), self.aslot(0, 8192), self.A_dram, "wld")
        GP.milestone("pool", "wld", self.c("wld"))
        pool_A = self.c("pool")
        bofs = self.bslot["l0ffn"]
        am = self.marks("dve")
        for oc in range(NIC):
            for tc in range(NTC):
                ts = slice(tc * 512, (tc + 1) * 512)
                bank = (oc * NTC + tc) % 4
                if oc == 0 and tc == 0:
                    TE.wait("pool", pool_A)
                    TE.wait("dve", self.c("dve"))
                am.wait(TE, 4)
                for ic in range(NIC):
                    TE.matmul(self.pb[bank][:, 0:512],
                              self.aslot(ic * 1024, 1024)[:, oc * 128:(oc + 1) * 128],
                              self.sA[ic][:, ts], start=(ic == 0), stop=(ic == NIC - 1))
                TE.inc("pe")
                VE.wait("pe", self.c("pe"))
                VE.scalar_tensor_tensor(self.x[oc][:, ts], self.pb[bank][:, 0:512],
                                        self.bias[:, bofs + oc:bofs + oc + 1],
                                        self.x[oc][:, ts], ALU.add, ALU.add)
                VE.inc("dve")
                am.mark()
        self.wdone((0, 8192))

    def mingru_layer(self, wz_name, wh_name, lidx, src, dst, bz_slot, bh_slot, rev):
        T, NTC = self.T, self.NTC
        GP, TE, VE, SC = self.GP, self.TE, self.VE, self.SC
        mgm = self.marks("dve")
        stage_dve = {}
        entry_dve = self.c("dve")
        zs = [self.otmp[:, 0:512], self.zst2[:, 0:512]]
        ts_ = [self.tmp2[:, 0:512], self.tst2[:, 0:512]]
        for oc in range(NIC):
            gr = oc // 2
            pb_ = gr % 2
            base = 4096 * pb_ + (oc % 2) * 2048
            wz_, wh_ = self.aslot(base, 1024), self.aslot(base + 1024, 1024)
            ws = "wld0" if pb_ == 0 else "wld1"
            if oc % 2 == 0:
                self.wload((4096 * pb_, 4096), self.aslot(4096 * pb_, 4096),
                           self.dram[wz_name][lidx * 4 + gr], ws)
            wld_here = (ws, self.c(ws))
            for tc in range(NTC):
                tsl = slice(tc * 512, (tc + 1) * 512)
                idx = oc * NTC + tc
                b0, b1 = 2 * (idx % 3), 2 * (idx % 3) + 1
                zt_, tt_ = zs[idx % 2], ts_[idx % 2]
                if tc == 0:
                    TE.wait(*wld_here)
                    if oc == 0:
                        TE.wait("dve", entry_dve)
                mgm.wait(TE, 3)
                for ic in range(NIC):
                    TE.matmul(self.pb[b0][:, 0:512], wz_[:, ic * 128:(ic + 1) * 128],
                              src[ic][:, tsl], start=(ic == 0), stop=(ic == NIC - 1))
                TE.inc("pe")
                for ic in range(NIC):
                    TE.matmul(self.pb[b1][:, 0:512], wh_[:, ic * 128:(ic + 1) * 128],
                              src[ic][:, tsl], start=(ic == 0), stop=(ic == NIC - 1))
                TE.inc("pe")
                SC.wait("pe", self.c("pe"))
                SC.wait("dve", stage_dve.get(idx - 2, entry_dve))
                SC.activation(zt_, self.pb[b0][:, 0:512], AF.Sigmoid,
                              bias=self.bias[:, bz_slot + oc:bz_slot + oc + 1])
                SC.inc("act")
                SC.activation(tt_, self.pb[b1][:, 0:512], AF.Tanh,
                              bias=self.bias[:, bh_slot + oc:bh_slot + oc + 1])
                SC.inc("act")
                VE.wait("act", self.c("act"))
                VE.wait("dve", self.c("dve"))
                VE.tensor_scalar(self.scr0[:, tsl], zt_, -1.0, 1.0, ALU.mult, ALU.add)
                VE.inc("dve")
                VE.wait("dve", self.c("dve"))
                VE.tensor_scalar(self.scr0[:, tsl], self.scr0[:, tsl], 1e-8, 1.0 - 1e-8,
                                 ALU.max, ALU.min)
                VE.inc("dve")
                VE.tensor_mul(self.bbf[:, tsl], zt_, tt_)
                VE.inc("dve")
                stage_dve[idx] = self.c("dve")
                mgm.mark()
            VE.wait("dve", self.c("dve"))
            if rev:
                VE.tensor_tensor_scan(dst[oc][:, ::-1], self.scr0[:, ::-1],
                                      self.bbf[:, ::-1], 0.0, ALU.mult, ALU.add)
            else:
                VE.tensor_tensor_scan(dst[oc][:], self.scr0[:], self.bbf[:],
                                      0.0, ALU.mult, ALU.add)
            VE.inc("dve")
            if lidx > 0:
                VE.wait("dve", self.c("dve"))
                VE.tensor_add(dst[oc][:], dst[oc][:], src[oc][:])
                VE.inc("dve")
            if oc % 2 == 1:
                self.wdone((4096 * pb_, 4096))
        self.ln(dst, dst, self.pat_l1a, src_f32=False)

    def ph_l1_attn(self):
        T, NTC = self.T, self.NTC
        GP, TE, VE, SC = self.GP, self.TE, self.VE, self.SC
        self.scr0 = self.salloc("scr0", [128, T], BF16)
        self.bbf = self.salloc("bbf", [128, T], BF16)
        self.zst2 = self.salloc("zst2", [128, 512], BF16)
        self.tst2 = self.salloc("tst2", [128, 512], BF16)
        self.ln(self.x, self.sA, self.pat_plain, src_f32=True)
        bufs = [self.sA, self.sB]
        for l in range(4):
            self.mingru_layer("fwd_wzh", None, l, bufs[l % 2], bufs[(l + 1) % 2],
                              self.bslot["fwd_bz"] + l * 8, self.bslot["fwd_bh"] + l * 8,
                              rev=False)
        self.ln(self.x, self.sB, self.pat_plain, src_f32=True)
        fofs = self.bslot["fus"]

        def evac_f(oc, tc, ts, bank):
            VE = self.VE
            VE.wait("pe", self.c("pe"))
            VE.scalar_tensor_tensor(self.x[oc][:, ts], self.pb[bank][:, 0:512],
                                    self.bias[:, fofs + oc:fofs + oc + 1],
                                    self.x[oc][:, ts], ALU.add, ALU.add)
            VE.inc("dve")
        self.proj("fus_top", self.sA, NIC, evac_f)
        bufs = [self.sB, self.sA]
        for l in range(4):
            self.mingru_layer("bwd_wzh", None, l, bufs[l % 2], bufs[(l + 1) % 2],
                              self.bslot["bwd_bz"] + l * 8, self.bslot["bwd_bh"] + l * 8,
                              rev=True)

        def evac_b(oc, tc, ts, bank):
            VE = self.VE
            VE.wait("pe", self.c("pe"))
            VE.tensor_add(self.x[oc][:, ts], self.x[oc][:, ts], self.pb[bank][:, 0:512])
            VE.inc("dve")
        self.proj("fus_bot", self.sB, NIC, evac_b)

    def ph_l1_ffn_attn(self):
        T, NTC, NCH = self.T, self.NTC, self.NCH
        GP, TE, VE, SC, SY = self.GP, self.TE, self.VE, self.SC, self.SY
        self.ln(self.x, self.sA, self.pat_ln1, src_f32=True)
        vt = [self.salloc(f"vt{i}", [128, 130], BF16) for i in range(self.NCH)]
        for i in range(self.NCH):
            VE.memset(vt[i][:, 64:65], 1.0)
            VE.memset(vt[i][:, 129:130], 1.0)
        VE.engine_nop()
        VE.inc("dve")
        qofs = self.bslot["bqkv"]
        for pair in range(NPAIR):
            pb_ = pair % 2
            wq, wk, wv = (self.aslot(3072 * pb_, 1024), self.aslot(3072 * pb_ + 1024, 1024),
                          self.aslot(3072 * pb_ + 2048, 1024))
            qt, kt, vtile = self.sC[0], self.sC[1], self.sC[2]
            ws = "wld0" if pb_ == 0 else "wld1"
            if pair >= 1:
                GP.wait("dve", self.c("dve"))
            self.wload((3072 * pb_, 3072), self.aslot(3072 * pb_, 3072), self.dram["wqkv3"][pair], ws)
            wld_here = (ws, self.c(ws))
            pm = self.marks("dve")
            for which, (wt, dst, bofs) in enumerate(
                    [(wq, qt, qofs + pair), (wk, kt, qofs + 8 + pair),
                     (wv, vtile, qofs + 16 + pair)]):
                for tc in range(NTC):
                    ts = slice(tc * 512, (tc + 1) * 512)
                    bank = (which * NTC + tc) % 4
                    if which == 0 and tc == 0:
                        TE.wait(*wld_here)
                        TE.wait("dve", self.c("dve"))
                    pm.wait(TE, 4)
                    for ic in range(NIC):
                        TE.matmul(self.pb[bank][:, 0:512], wt[:, ic * 128:(ic + 1) * 128],
                                  self.sA[ic][:, ts], start=(ic == 0), stop=(ic == NIC - 1))
                    TE.inc("pe")
                    VE.wait("pe", self.c("pe"))
                    VE.tensor_scalar(dst[:, ts], self.pb[bank][:, 0:512],
                                     self.bias[:, bofs:bofs + 1], None, ALU.add)
                    VE.inc("dve")
                    pm.mark()
            self.wdone((3072 * pb_, 3072))
            c_steps = []
            for which, t_ in ((0, qt), (1, kt)):
                for tc in range(NTC):
                    def cstep(which=which, t_=t_, tc=tc):
                        ts = slice(tc * 512, (tc + 1) * 512)
                        SC.wait("dve", self.c("dve"))
                        SC.wait("pe", self.c("pe"))
                        SC.activation(self.sq0[:, 0:512], t_[:, ts], AF.Square)
                        SC.inc("act")
                        TE.wait("act", self.c("act"))
                        TE.wait("dve", self.c("dve"))
                        TE.matmul(self.pb[5][0:1, 0:512], self.ones_col[:, 0:1],
                                  self.sq0[:, 0:512], start=True, stop=True)
                        TE.inc("pe")
                        VE.wait("pe", self.c("pe"))
                        if tc == 0:
                            VE.reduce_max(self.rw0[0:1, which:which + 1],
                                          self.pb[5][0:1, 0:512], axis=AX.X)
                            VE.inc("dve")
                        else:
                            VE.reduce_max(self.rw0[0:1, 2:3], self.pb[5][0:1, 0:512], axis=AX.X)
                            VE.inc("dve")
                            VE.wait("dve", self.c("dve"))
                            VE.tensor_max(self.rw0[0:1, which:which + 1],
                                          self.rw0[0:1, which:which + 1], self.rw0[0:1, 2:3])
                            VE.inc("dve")
                    c_steps.append(cstep)
            ci = 0
            for kc in range(NCH):
                cs = slice(kc * 128, (kc + 1) * 128)
                tb = kc % 2
                TE.wait("dve", self.c("dve"))
                TE.transpose(self.pt[tb][:, 0:128], vtile[:, cs], self.ident[:])
                TE.inc("pe")
                VE.wait("pe", self.c("pe"))
                VE.tensor_copy(vt[kc][:, 0:64], self.pt[tb][:, 0:64])
                VE.tensor_copy(vt[kc][:, 65:129], self.pt[tb][:, 64:128])
                VE.inc("dve")
                if kc % 2 == 1 and ci < len(c_steps):
                    c_steps[ci]()
                    ci += 1
            while ci < len(c_steps):
                c_steps[ci]()
                ci += 1
            SC.wait("dve", self.c("dve"))
            SC.activation(self.rw1[0:1, 0:2], self.rw0[0:1, 0:2], AF.Ln)
            SC.inc("act")
            VE.wait("act", self.c("act"))
            VE.tensor_add(self.rw0[0:1, 3:4], self.rw1[0:1, 0:1], self.rw1[0:1, 1:2])
            VE.inc("dve")
            SC.wait("dve", self.c("dve"))
            SC.activation(self.rw0[0:1, 4:5], self.rw0[0:1, 3:4], AF.Exp, scale=0.5)
            SC.inc("act")
            VE.wait("act", self.c("act"))
            VE.tensor_scalar_mul(self.c1b[0:1, 0:1], self.rw0[0:1, 4:5], -0.125)
            VE.inc("dve")
            TE.wait("dve", self.c("dve"))
            TE.matmul(self.pb[5][:, 0:1], self.ones_row[0:1, :], self.c1b[0:1, 0:1],
                      start=True, stop=True)
            TE.inc("pe")
            VE.wait("pe", self.c("pe"))
            VE.tensor_copy(self.ccol[:, 0:1], self.pb[5][:, 0:1])
            VE.inc("dve")
            ccol_ready0 = self.c("dve")
            for h in range(2):
                hs = slice(64 * h, 64 * h + 64)
                ccol_ready = ccol_ready0
                denom_pending = []

                def emit_denom(tc):
                    ts2 = slice(tc * 512, (tc + 1) * 512)
                    pvb = self.pb[4 if tc % 2 == 0 else 2]
                    bcb = self.pb[5 if tc % 2 == 0 else 3]
                    VE.wait("pe", pv_done[tc])
                    VE.reciprocal(self.crow[0:1, 0:512], pvb[64:65, 0:512])
                    VE.inc("dve")
                    VE.wait("dve", self.c("dve"))
                    VE.tensor_copy(self.rstd_bf[0:1, 0:512], self.crow[0:1, 0:512])
                    VE.inc("dve")
                    TE.wait("dve", self.c("dve"))
                    TE.matmul(bcb[0:64, 0:512], self.ones_row[0:1, 0:64],
                              self.rstd_bf[0:1, 0:512], start=True, stop=True)
                    TE.inc("pe")
                    SC.wait("pe", self.c("pe"))
                    SC.wait("dve", self.c("dve"))
                    SC.copy(self.otmp[0:64, 0:512], pvb[0:64, 0:512])
                    SC.inc("act")
                    VE.wait("act", self.c("act"))
                    VE.tensor_tensor(self.sB[pair][hs, ts2], self.otmp[0:64, 0:512],
                                     bcb[0:64, 0:512], ALU.mult)
                    VE.inc("dve")

                pv_done = {}
                for tc in range(NTC):
                    ts = slice(tc * 512, (tc + 1) * 512)
                    pvb = self.pb[4 if tc % 2 == 0 else 2]
                    s_marks = []
                    pv_marks = []
                    exp_marks = []

                    def emit_S(kc):
                        cs = slice(kc * 128, (kc + 1) * 128)
                        bank = kc % 2
                        if kc >= 2:
                            TE.wait("act", exp_marks[kc - 2])
                        TE.matmul(self.pb[bank][:, 0:512], kt[hs, cs], qt[hs, ts],
                                  start=True, stop=True)
                        TE.inc("pe")
                        s_marks.append(self.c("pe"))

                    def emit_exp(kc):
                        bank = kc % 2
                        ptile = self.sq0 if kc % 2 == 0 else self.sq1
                        SC.wait("pe", s_marks[kc])
                        if kc == 0:
                            SC.wait("dve", ccol_ready)
                        if kc >= 2:
                            SC.wait("pe", pv_marks[kc - 2])
                        SC.activation(ptile[:, 0:512], self.pb[bank][:, 0:512], AF.Exp,
                                      scale=0.125, bias=self.ccol[:, 0:1])
                        SC.inc("act")
                        exp_marks.append(self.c("act"))

                    def emit_PV(kc):
                        ptile = self.sq0 if kc % 2 == 0 else self.sq1
                        TE.wait("act", exp_marks[kc])
                        if kc == 0 and tc >= 2:
                            TE.wait("dve", self.c("dve"))
                        TE.matmul(pvb[0:65, 0:512], vt[kc][:, 65 * h:65 * h + 65],
                                  ptile[:, 0:512], start=(kc == 0), stop=(kc == NCH - 1))
                        TE.inc("pe")
                        pv_marks.append(self.c("pe"))

                    emit_S(0)
                    emit_S(1)
                    emit_exp(0)
                    for kc in range(NCH - 2):
                        emit_PV(kc)
                        emit_S(kc + 2)
                        emit_exp(kc + 1)
                    emit_PV(NCH - 2)
                    emit_exp(NCH - 1)
                    emit_PV(NCH - 1)
                    pv_done[tc] = self.c("pe")
                    if denom_pending:
                        emit_denom(denom_pending.pop())
                    denom_pending.append(tc)
                if denom_pending:
                    emit_denom(denom_pending.pop())
        bofs = self.bslot["bo"]

        def evac_o(oc, tc, ts, bank):
            VE = self.VE
            VE.wait("pe", self.c("pe"))
            VE.scalar_tensor_tensor(self.x[oc][:, ts], self.pb[bank][:, 0:512],
                                    self.bias[:, bofs + oc:bofs + oc + 1],
                                    self.x[oc][:, ts], ALU.add, ALU.add)
            VE.inc("dve")
        self.proj("wo2", self.sB, NIC, evac_o)
        self.bar()
        self.ln(self.x, self.sB, self.pat_ln2, src_f32=True)
        for ic in range(NIC):
            GP.dma_start(out=self.gatew[:, ic * 8:(ic + 1) * 8],
                         in_=self.dram["gate_pack"][ic])
            GP.inc("wld", 16)
        GP.milestone("pool", "wld", self.c("wld"))
        gw_ready = self.c("pool")
        for tc in range(NTC):
            ts = slice(tc * 512, (tc + 1) * 512)
            TE.wait("pool", gw_ready)
            TE.wait("dve", self.c("dve"))
            for ic in range(NIC):
                TE.matmul(self.pb[0][0:8, 0:512], self.gatew[:, ic * 8:(ic + 1) * 8],
                          self.sB[ic][:, ts], start=(ic == 0), stop=(ic == NIC - 1))
            TE.inc("pe")
            VE.wait("pe", self.c("pe"))
            VE.tensor_scalar(self.sq0[0:8, 0:512], self.pb[0][0:8, 0:512],
                             self.gateb[:, 0:1], None, ALU.add)
            VE.inc("dve")
            for c4 in range(4):
                cs = slice(c4 * 128, (c4 + 1) * 128)
                chunk = tc * 4 + c4
                TE.wait("dve", self.c("dve"))
                TE.transpose(self.pt[0][:, 0:8], self.sq0[0:8, c4 * 128:(c4 + 1) * 128],
                             self.ident[0:8, 0:8])
                TE.inc("pe")
                VE.wait("pe", self.c("pe"))
                VE.tensor_copy(self.otmp[:, 0:8], self.pt[0][:, 0:8])
                VE.inc("dve")
                VE.wait("dve", self.c("dve"))
                VE.reduce_max(self.otmp[:, 8:9], self.otmp[:, 0:8], axis=AX.X)
                VE.inc("dve")
                VE.wait("dve", self.c("dve"))
                VE.tensor_scalar(self.otmp[:, 16:24], self.otmp[:, 0:8],
                                 self.otmp[:, 8:9], None, ALU.is_equal)
                VE.inc("dve")
                VE.wait("dve", self.c("dve"))
                VE.scalar_tensor_tensor(self.otmp[:, 24:32], self.otmp[:, 16:24], -1e30,
                                        self.otmp[:, 0:8], ALU.mult, ALU.add)
                VE.inc("dve")
                VE.wait("dve", self.c("dve"))
                VE.reduce_max(self.otmp[:, 9:10], self.otmp[:, 24:32], axis=AX.X)
                VE.inc("dve")
                VE.wait("dve", self.c("dve"))
                VE.tensor_scalar(self.otmp[:, 32:40], self.otmp[:, 24:32],
                                 self.otmp[:, 9:10], None, ALU.is_equal)
                VE.inc("dve")
                VE.wait("dve", self.c("dve"))
                VE.tensor_sub(self.otmp[:, 10:11], self.otmp[:, 8:9], self.otmp[:, 9:10])
                VE.inc("dve")
                SC.wait("dve", self.c("dve"))
                SC.activation(self.otmp[:, 11:12], self.otmp[:, 10:11], AF.Sigmoid)
                SC.inc("act")
                VE.wait("act", self.c("act"))
                VE.tensor_scalar(self.otmp[:, 12:13], self.otmp[:, 11:12], -1.0, 1.0,
                                 ALU.mult, ALU.add)
                VE.inc("dve")
                VE.wait("dve", self.c("dve"))
                VE.tensor_scalar_mul(self.otmp[:, 16:24], self.otmp[:, 16:24],
                                     self.otmp[:, 11:12])
                VE.inc("dve")
                VE.wait("dve", self.c("dve"))
                VE.tensor_scalar_mul(self.otmp[:, 32:40], self.otmp[:, 32:40],
                                     self.otmp[:, 12:13])
                VE.inc("dve")
                VE.wait("dve", self.c("dve"))
                VE.tensor_add(self.otmp[:, 16:24], self.otmp[:, 16:24], self.otmp[:, 32:40])
                VE.inc("dve")
                VE.wait("dve", self.c("dve"))
                VE.tensor_copy(self.sq1[:, 0:8], self.otmp[:, 16:24])
                VE.inc("dve")
                TE.wait("dve", self.c("dve"))
                TE.transpose(self.pt[1][0:8, 0:128], self.sq1[:, 0:8], self.ident[:])
                TE.inc("pe")
                VE.wait("pe", self.c("pe"))
                VE.tensor_copy(self.ew8[0:8, chunk * 128:(chunk + 1) * 128],
                               self.pt[1][0:8, 0:128])
                VE.inc("dve")

    def ph_compact(self):
        T = self.T
        GP, TE, VE, SC, SY = self.GP, self.TE, self.VE, self.SC, self.SY
        pos = self.salloc("cp_pos", [8, T], F32)[0:8, :]
        slotm = self.salloc("cp_slotm", [8, T], F32)[0:8, :]
        mfp = self.sA[3][0:8, :]
        one8 = self.sA[2][0:8, :]
        markt = self.sA[4][0:8, :]
        isov = self.sA[5][0:8, :]
        i1T = self.salloc("cp_i1T", [128, 16], F32)
        i2T = self.salloc("cp_i2T", [128, 16], F32)
        waT = self.salloc("cp_waT", [128, 16], F32)
        wbT = self.salloc("cp_wbT", [128, 16], F32)
        slT = self.salloc("cp_slT", [128, 8], F32)
        ewTt = self.salloc("cp_ewTt", [128, 8], BF16)
        eqt = self.salloc("cp_eqt", [128, 8], F32)
        t2t = self.salloc("cp_t2t", [128, 8], F32)
        wsumc = self.salloc("cp_wsum", [128, 1], F32)
        cmp8 = self.salloc("cp_cmp8", [8, 16], F32)
        cmpb = self.salloc("cp_cmpb", [8, 16], BF16)
        teb = self.salloc("cp_teb", [1, 16], BF16)
        wqf = self.salloc("cp_wqf", [128, 16], F32)
        wqg = self.salloc("cp_wqg", [128, 16], F32)
        prestg = [self.sA[0][:, 0:1024], self.sA[1][:, 0:1024]]
        wab = self.salloc("cp_wab", [128, 32], F32)
        scat = self.arena[:, 0:8192].rearrange("p (a b) -> p a b", b=1024)
        self.identf = self.salloc("cp_identf", [128, 128], F32)
        GP.dma_start(out=self.identf[:], in_=self.dram["c_identf"])
        GP.inc("wld", 16)
        idf_wld = self.c("wld")
        TE.wait("wld", idf_wld)
        VE.wait("dve", self.c("dve"))
        VE.memset(one8, 1.0)
        VE.inc("dve")
        VE.tensor_scalar(mfp, self.ew8[0:8, :], 0.0, None, ALU.not_equal)
        VE.inc("dve")
        VE.wait("dve", self.c("dve"))
        VE.wait("dve", self.c("dve"))
        VE.tensor_tensor_scan(pos, one8, mfp, 0.0, ALU.mult, ALU.add)
        VE.inc("dve")
        VE.wait("dve", self.c("dve"))
        VE.tensor_scalar(self.ccol[0:8, 2:3], pos[:, T - 1:T], 1025.0, None, ALU.is_ge)
        VE.inc("dve")
        TE.wait("dve", self.c("dve"))
        TE.transpose(self.pb[0][0:1, 0:8], self.ccol[0:8, 2:3], self.identf[0:8, 0:8])
        TE.inc("pe")
        VE.wait("pe", self.c("pe"))
        VE.tensor_copy(self.rw0[0:1, 0:8], self.pb[0][0:1, 0:8])
        VE.inc("dve")
        VE.wait("dve", self.c("dve"))
        VE.tensor_tensor_scan(self.rw0[0:1, 8:16], self.ones_row[0:1, 0:8],
                              self.rw0[0:1, 0:8], 0.0, ALU.mult, ALU.add)
        VE.inc("dve")
        VE.wait("dve", self.c("dve"))
        VE.tensor_sub(self.rw0[0:1, 16:24], self.rw0[0:1, 8:16], self.rw0[0:1, 0:8])
        VE.inc("dve")
        TE.wait("dve", self.c("dve"))
        TE.transpose(self.pb[1][0:8, 0:1], self.rw0[0:1, 16:24], self.identf[0:1, 0:1])
        TE.inc("pe")
        VE.wait("pe", self.c("pe"))
        VE.tensor_copy(self.ccol[0:8, 3:4], self.pb[1][0:8, 0:1])
        VE.inc("dve")
        VE.wait("dve", self.c("dve"))
        VE.tensor_scalar(self.ccol[0:8, 5:6], self.ccol[0:8, 2:3], -99.0, 99.0,
                         ALU.mult, ALU.add)
        VE.inc("dve")
        VE.wait("dve", self.c("dve"))
        VE.tensor_add(self.ccol[0:8, 5:6], self.ccol[0:8, 5:6], self.ccol[0:8, 3:4])
        VE.inc("dve")
        VE.wait("dve", self.c("dve"))
        VE.tensor_scalar(cmp8[0:8, 0:3], self.irow[0:8, 0:3], self.ccol[0:8, 5:6],
                         None, ALU.is_ge)
        VE.inc("dve")
        VE.wait("dve", self.c("dve"))
        VE.tensor_copy(cmpb[0:8, 0:3], cmp8[0:8, 0:3])
        VE.inc("dve")
        TE.wait("dve", self.c("dve"))
        TE.matmul(self.pb[2][0:1, 0:3], self.ones_col[0:8, 0:1], cmpb[0:8, 0:3],
                  start=True, stop=True)
        TE.inc("pe")
        VE.wait("pe", self.c("pe"))
        VE.tensor_scalar(self.rw0[0:1, 32:35], self.pb[2][0:1, 0:3], -1.0, None, ALU.add)
        VE.inc("dve")
        VE.wait("dve", self.c("dve"))
        VE.tensor_scalar_min(self.rw0[0:1, 32:35], self.rw0[0:1, 32:35], 7.0)
        VE.inc("dve")
        VE.wait("dve", self.c("dve"))
        VE.tensor_scalar(self.rw0[0:1, 36:39], self.rw0[0:1, 32:35], 0.0, None, ALU.max)
        VE.inc("dve")
        VE.wait("dve", self.c("dve"))
        VE.tensor_copy(teb[0:1, 0:3], self.rw0[0:1, 36:39])
        VE.inc("dve")
        TE.wait("dve", self.c("dve"))
        TE.matmul(self.pb[3][:, 0:3], self.ones_row[0:1, :], teb[0:1, 0:3],
                  start=True, stop=True)
        TE.inc("pe")
        VE.wait("pe", self.c("pe"))
        VE.tensor_scalar(wqf[:, 0:3], self.pb[3][:, 0:3], 512.0, None, ALU.mult)
        VE.inc("dve")
        for g in range(4):
            VE.wait("dve", self.c("dve"))
            VE.tensor_scalar(wqg[:, 0:3], wqf[:, 0:3], self.pgt[:, g:g + 1],
                             None, ALU.add)
            VE.inc("dve")
            VE.wait("dve", self.c("dve"))
            VE.tensor_copy(self.wqidx[:, g * 3:(g + 1) * 3], wqg[:, 0:3])
            VE.inc("dve")
        ovf = self.salloc("cp_ovf", [128, 64], F32)
        for k in range(3):
            VE.wait("dve", self.c("dve"))
            VE.tensor_scalar(wqg[:, 4:5], self.pb[3][:, k:k + 1], 1024.0, 16384.0,
                             ALU.mult, ALU.add)
            VE.inc("dve")
            VE.wait("dve", self.c("dve"))
            VE.tensor_scalar(ovf[:], self.ovc[:], wqg[:, 4:5], None, ALU.add)
            VE.inc("dve")
            VE.wait("dve", self.c("dve"))
            VE.tensor_copy(self.ovit[k][:], ovf[:])
            VE.inc("dve")
        self.wqidx_dve = self.c("dve")
        VE.wait("dve", self.c("dve"))
        VE.tensor_scalar(slotm, pos, self.ebase[0:8, 0:1], None, ALU.add)
        VE.inc("dve")
        VE.tensor_scalar(isov, pos, 1025.0, None, ALU.is_ge)
        VE.inc("dve")
        VE.wait("dve", self.c("dve"))
        VE.scalar_tensor_tensor(slotm, isov, self.ebase[0:8, 1:2], slotm,
                                ALU.mult, ALU.add)
        VE.inc("dve")
        VE.tensor_scalar(markt, mfp, -32256.0, 32256.0, ALU.mult, ALU.add)
        VE.inc("dve")
        VE.wait("dve", self.c("dve"))
        VE.tensor_add(slotm, slotm, markt)
        VE.inc("dve")
        for tb in range(self.NCH):
            cs = slice(tb * 128, (tb + 1) * 128)
            TE.wait("dve", self.c("dve"))
            TE.transpose(self.pb[4 + tb % 2][:, 0:8], slotm[0:8, cs], self.identf[0:8, 0:8])
            TE.transpose(self.pt[tb % 2][:, 0:8], self.ew8[0:8, cs], self.ident[0:8, 0:8])
            TE.inc("pe")
            VE.wait("pe", self.c("pe"))
            VE.tensor_copy(slT[:], self.pb[4 + tb % 2][:, 0:8])
            VE.tensor_copy(ewTt[:], self.pt[tb % 2][:, 0:8])
            VE.inc("dve")
            VE.wait("dve", self.c("dve"))
            VE.tensor_reduce(i1T[:, tb:tb + 1], slT[:], AX.X, ALU.min)
            VE.inc("dve")
            VE.wait("dve", self.c("dve"))
            VE.tensor_scalar(eqt[:], slT[:], i1T[:, tb:tb + 1], None, ALU.is_equal)
            VE.inc("dve")
            VE.wait("dve", self.c("dve"))
            VE.scalar_tensor_tensor(t2t[:], eqt[:], 1e6, slT[:], ALU.mult, ALU.add)
            VE.inc("dve")
            VE.wait("dve", self.c("dve"))
            VE.tensor_reduce(i2T[:, tb:tb + 1], t2t[:], AX.X, ALU.min)
            VE.inc("dve")
            VE.wait("dve", self.c("dve"))
            VE.tensor_scalar_min(i2T[:, tb:tb + 1], i2T[:, tb:tb + 1], 24576.0)
            VE.inc("dve")
            VE.tensor_tensor(t2t[:], ewTt[:], eqt[:], ALU.mult)
            VE.inc("dve")
            VE.wait("dve", self.c("dve"))
            VE.reduce_sum(waT[:, tb:tb + 1], t2t[:], axis=AX.X)
            VE.inc("dve")
            VE.reduce_sum(wsumc[:], ewTt[:], axis=AX.X)
            VE.inc("dve")
            VE.wait("dve", self.c("dve"))
            VE.tensor_sub(wbT[:, tb:tb + 1], wsumc[:], waT[:, tb:tb + 1])
            VE.inc("dve")
        sb16a = self.salloc("cp_sb16a", [16, 128], F32)
        sb16b = self.salloc("cp_sb16b", [16, 128], F32)
        TE.wait("dve", self.c("dve"))
        TE.transpose(self.pb[0][0:16, 0:128], i1T[:], self.identf[:])
        TE.transpose(self.pb[1][0:16, 0:128], i2T[:], self.identf[:])
        TE.inc("pe")
        VE.wait("pe", self.c("pe"))
        VE.tensor_copy(sb16a[:], self.pb[0][0:16, 0:128])
        VE.inc("dve")
        VE.wait("dve", self.c("dve"))
        VE.tensor_copy(sb16b[:], self.pb[1][0:16, 0:128])
        VE.inc("dve")
        GP.wait("dve", self.c("dve"))
        GP.dma_start(out=self.idx_dram[0], in_=sb16a[:])
        GP.inc("lb", 16)
        GP.dma_start(out=self.idx_dram[1], in_=sb16b[:])
        GP.inc("lb", 16)
        GP.wait("lb", self.c("lb"))
        for lane in range(2):
            for k in range(8):
                GP.dma_start(out=self.idxt[lane][16 * k:16 * (k + 1), :],
                             in_=self.idx_dram[lane])
                GP.inc("lb", 16)
        idx_ready = self.c("lb")
        idx_pe_free = self.c("pe")
        h2_entry = self.c("dve")
        stm = self.marks("hz")
        for tb in range(self.NCH):
            cs = slice(tb * 128, (tb + 1) * 128)
            par = tb % 2
            TE.wait("dve", self.c("dve"))
            stm.wait(TE, 2)
            for fc in range(NIC):
                TE.transpose(self.pt[par][:, fc * 128:(fc + 1) * 128],
                             self.sB[fc][:, cs], self.ident[:])
            TE.inc("pe")
            VE.wait("pe", self.c("pe"))
            VE.tensor_copy(prestg[par][:, 0:1024], self.pt[par][:, 0:1024])
            VE.inc("dve")
            SY.wait("dve", self.c("dve"))
            SY.dma_start(out=self.h2t_dram[tb * 128:(tb + 1) * 128, :],
                         in_=prestg[par][:])
            SY.inc("hz", 16)
            stm.mark()
        VE.wait("dve", self.c("dve"))
        VE.tensor_copy(wab[:].rearrange("p (a b) -> p a b", b=2)[:, :, 0], waT[:])
        VE.inc("dve")
        VE.wait("dve", self.c("dve"))
        VE.tensor_copy(wab[:].rearrange("p (a b) -> p a b", b=2)[:, :, 1], wbT[:])
        VE.inc("dve")
        SY.wait("dve", self.c("dve"))
        SY.dma_start(out=self.wab_dram.rearrange("(tb p) w -> p tb w", p=128),
                     in_=wab[:].rearrange("p (a b) -> p a b", b=2))
        SY.inc("hz", 16)
        h2t_done = self.c("hz")
        h3 = self.h2t_dram.rearrange("(b r) f -> b r f", b=16)
        for half in range(2):
            SY.wait("hz", h2t_done)
            if half >= 1:
                SY.wait("lb", self.c("lb"))
            for a in range(8):
                SY.dma_start(out=scat[16 * a:16 * (a + 1), :, :],
                             in_=h3[:, a + 64 * half:a + 64 * half + 57:8, :])
                SY.inc("ld", 16)
            GP.wait("ld", self.c("ld"))
            GP.wait("lb", idx_ready)
            GP.wait("hz", self.hz_zero)
            GP.dma_scatter_add(self.hsort, scat[:, :, :],
                               self.idxt[0][:, 64 * half:64 * half + 64],
                               1024, 1024, 1024)
            GP.inc("lb", 16)
            GP.dma_scatter_add(self.hsort, scat[:, :, :],
                               self.idxt[1][:, 64 * half:64 * half + 64],
                               1024, 1024, 1024)
            GP.inc("lb", 16)
        self.scat_done = self.c("lb")
        TE.wait("pe", idx_pe_free)
        self.bar()

    def ph_moe(self):
        T = self.T
        GP, TE, VE, SC, SY = self.GP, self.TE, self.VE, self.SC, self.SY
        wq = [self.salloc(f"mq_wq{i}", [128, WQEL], BF16) for i in range(2)]
        Bh = self.salloc("mq_Bh", [128, 8192], BF16)
        Bacc = self.salloc("mq_Bacc", [128, 8192], BF16)
        hfm = self.salloc("mq_hfm", [128, 8192], BF16)
        outstg = self.salloc("mq_outstg", [128, 2, 1024], BF16)
        ovstg = hfm_ov = None
        b1f = self.salloc("mq_b1f", [128, 8], F32)
        wabG = self.salloc("mq_wabG", [128, 16, 2], F32)
        self.b2sb = self.salloc("mq_b2sb", [8, 1024], BF16)
        GP.dma_start(out=self.b2sb[:], in_=self.dram["b2p"])
        GP.inc("wld", 16)
        b2_wld = self.c("wld")
        hid = self.arena
        ovstg = Bh[:].rearrange("p (a b) -> p a b", b=1024)
        IOA = bass.IndirectOffsetOnAxis
        q_pe = {}
        tile_st = {}
        wq_wld = {}
        m1 = self.marks("act")
        m2 = self.marks("dve")
        mo = self.marks("st")
        def gather_q(q):
            i, g = q // 4, q % 4
            par = q % 2
            ws = "wld0" if par == 0 else "wld1"
            if q == 0:
                GP.wait("dve", self.wqidx_dve)
            if q >= 2:
                GP.wait("pe", q_pe[q - 2])
            r0 = (i * 4 + g) * 128
            if i < 8:
                GP.dma_start(out=wq[par][:, 0:8256],
                             in_=self.dram["wsrc"][r0:r0 + 128, 0:8256])
                GP.inc(ws, 16)
                GP.dma_start(out=wq[par][:, 8256:WQEL],
                             in_=self.dram["wsrc"][r0:r0 + 128, 8256:WQEL])
            else:
                k = i - 8
                GP.indirect_dma_start(
                    out=wq[par][:, 0:8256], out_offset=None,
                    in_=self.dram["wsrc"],
                    in_offset=IOA(self.wqidx[:, g * 3 + k:g * 3 + k + 1], 0),
                    bounds_check=4095, oob_is_err=False)
                GP.inc(ws, 16)
                GP.indirect_dma_start(
                    out=wq[par][:, 8256:WQEL], out_offset=None,
                    in_=self.dram["wsrc"],
                    in_offset=IOA(self.wqidx[:, g * 3 + k:g * 3 + k + 1], 0),
                    element_offset=8256,
                    bounds_check=4095, oob_is_err=False)
            GP.inc(ws, 16)
            wq_wld[q] = (ws, self.c(ws))

        gather_q(0)
        gather_q(1)
        tr_pe = {}
        wf_dves = {}
        for i in range(NTILE):
            SY.wait("lb", self.scat_done)
            if i >= 1:
                SY.wait("pe", tr_pe[i - 1])
            if i >= 9:
                SY.wait("st", tile_st[i - 1])
            if i < 8:
                for cc in range(8):
                    SY.dma_start(
                        out=Bh[:, cc * 1024:(cc + 1) * 1024],
                        in_=self.hsort[i * 2048 + cc * 128:i * 2048 + (cc + 1) * 128, :])
                    SY.inc("ld", 16)
            else:
                GP.wait("lb", self.scat_done)
                GP.wait("pe", tr_pe[i - 1] if i >= 1 else 0)
                GP.wait("st", tile_st[i - 1])
                GP.dma_gather(Bh[:].rearrange("p (a b) -> p a b", b=1024),
                              self.hsort, self.ovit[i - 8][:], 1024, 1024, 1024)
                GP.inc("ld", 16)
            h_ld = self.c("ld")
            for cc in range(8):
                TE.wait("ld", h_ld)
                if cc >= 2:
                    TE.wait("dve", self.c("dve"))
                for fc in range(NIC):
                    TE.transpose(self.pt[cc % 2][:, fc * 128:(fc + 1) * 128],
                                 Bh[:, cc * 1024 + fc * 128:cc * 1024 + (fc + 1) * 128],
                                 self.ident[:])
                TE.inc("pe")
                VE.wait("pe", self.c("pe"))
                VE.tensor_copy(
                    hfm[:].rearrange("p (a b) -> p a b", a=8)[:, :, cc * 128:(cc + 1) * 128],
                    self.pt[cc % 2][:].rearrange("p (a b) -> p a b", a=8))
                VE.inc("dve")
            tr_pe[i] = self.c("pe")
            for g in range(4):
                q = i * 4 + g
                par = q % 2
                TE.wait(*wq_wld[q])
                VE.wait(*wq_wld[q])
                VE.tensor_copy(b1f[:], wq[par][:, 8192:8200])
                VE.inc("dve")
                b1_dve = self.c("dve")
                ph2_prev = self.c("pe")
                for tc2 in range(2):
                    for oc in range(NIC):
                        bank = oc % 2
                        m1.wait(TE, 2)
                        for ic in range(NIC):
                            TE.matmul(self.pb[bank][:, 0:512],
                                      wq[par][:, ic * 1024 + oc * 128:ic * 1024 + (oc + 1) * 128],
                                      hfm[:, ic * 1024 + tc2 * 512:ic * 1024 + (tc2 + 1) * 512],
                                      start=(ic == 0), stop=(ic == NIC - 1))
                        TE.inc("pe")
                        SC.wait("pe", self.c("pe"))
                        SC.wait("dve", b1_dve)
                        if g > 0 and tc2 == 0 and oc < 2:
                            SC.wait("pe", ph2_prev)
                        SC.activation(hid[:, tc2 * 4096 + oc * 512:tc2 * 4096 + (oc + 1) * 512],
                                      self.pb[bank][:, 0:512], AF.Relu,
                                      bias=b1f[:, oc:oc + 1])
                        SC.inc("act")
                        m1.mark()
                    act_hid = self.c("act")
                    for oc2 in range(NIC):
                        bank = 2 + oc2 % 2
                        if oc2 == 0:
                            TE.wait("act", act_hid)
                        m2.wait(TE, 2)
                        for kc in range(NIC):
                            TE.matmul(self.pb[bank][:, 0:512],
                                      wq[par][:, 8200 + kc * 1024 + oc2 * 128:8200 + kc * 1024 + (oc2 + 1) * 128],
                                      hid[:, tc2 * 4096 + kc * 512:tc2 * 4096 + (kc + 1) * 512],
                                      start=(kc == 0), stop=(kc == NIC - 1))
                        TE.inc("pe")
                        VE.wait("pe", self.c("pe"))
                        acc_v = Bacc[:, oc2 * 1024 + tc2 * 512:oc2 * 1024 + (tc2 + 1) * 512]
                        if g == 0:
                            VE.tensor_copy(acc_v, self.pb[bank][:, 0:512])
                        else:
                            VE.wait("dve", self.c("dve"))
                            VE.tensor_tensor(acc_v, acc_v, self.pb[bank][:, 0:512], ALU.add)
                        VE.inc("dve")
                        m2.mark()
                q_pe[q] = self.c("pe")
                if q + 2 < NTILE * 4:
                    gather_q(q + 2)
            acc = Bacc
            for cc in range(8):
                TE.wait("dve", self.c("dve"))
                mo.wait(TE, 2)
                for fc in range(NIC):
                    TE.transpose(self.pt[cc % 2][:, fc * 128:(fc + 1) * 128],
                                 acc[:, fc * 1024 + cc * 128:fc * 1024 + (cc + 1) * 128],
                                 self.ident[:])
                TE.inc("pe")
                VE.wait("pe", self.c("pe"))
                if i < 8:
                    VE.tensor_copy(outstg[:, cc % 2, :], self.pt[cc % 2][:, 0:1024])
                    VE.inc("dve")
                    SY.wait("dve", self.c("dve"))
                    SY.dma_start(
                        out=self.osort[i * 2048 + cc * 128:i * 2048 + (cc + 1) * 128, :],
                        in_=outstg[:, cc % 2, :])
                    SY.inc("st", 16)
                else:
                    VE.tensor_copy(ovstg[:, cc, :], self.pt[cc % 2][:, 0:1024])
                    VE.inc("dve")
                mo.mark()
            if i >= 8:
                GP.wait("dve", self.c("dve"))
                GP.dma_scatter_add(self.osort, ovstg[:, :, :], self.ovit[i - 8][:],
                                   1024, 1024, 1024)
                GP.inc("st", 16)
            tile_st[i] = self.c("st")
        mg1 = wq[0][:, 0:8192].rearrange("p (a b) -> p a b", a=8)
        mg2 = wq[0][:, 8192:16384].rearrange("p (a b) -> p a b", a=8)
        msum = wq[1][:, 0:8192].rearrange("p (a b) -> p a b", a=8)
        mg3 = self.mg_dram.rearrange("(b r) f -> b r f", b=16)
        w3 = self.wab_dram.rearrange("(b r) w -> b r w", b=16)
        for half in range(2):
            GP.wait("st", tile_st[NTILE - 1])
            GP.wait("pe", q_pe[NTILE * 4 - 1])
            if half == 1:
                GP.wait("st", self.c("st"))
            for a in range(8):
                SY.dma_start(out=wabG[16 * a:16 * (a + 1), 0:8, :],
                             in_=w3[:, a + 64 * half:a + 64 * half + 57:8, :])
                SY.inc("ld", 16)
            wab_ld = self.c("ld")
            GP.dma_gather(mg1, self.osort, self.idxt[0][:, 64 * half:64 * half + 64],
                          1024, 1024, 1024)
            GP.inc("lb", 16)
            GP.dma_gather(mg2, self.osort, self.idxt[1][:, 64 * half:64 * half + 64],
                          1024, 1024, 1024)
            GP.inc("lb", 16)
            VE.wait("lb", self.c("lb"))
            VE.wait("ld", wab_ld)
            VE.wait("dve", self.c("dve"))
            for cc in range(8):
                VE.tensor_scalar(msum[:, cc, :], mg1[:, cc, :],
                                 wabG[:, cc, 0:1], None, ALU.mult)
                VE.inc("dve")
                VE.wait("dve", self.c("dve"))
                VE.scalar_tensor_tensor(msum[:, cc, :], mg2[:, cc, :],
                                        wabG[:, cc, 1:2], msum[:, cc, :],
                                        ALU.mult, ALU.add)
                VE.inc("dve")
            VE.wait("dve", self.c("dve"))
            SY.wait("dve", self.c("dve"))
            for a in range(8):
                SY.dma_start(out=mg3[:, a + 64 * half:a + 64 * half + 57:8, :],
                             in_=msum[16 * a:16 * (a + 1), :, :])
                SY.inc("st", 16)
        unwrap_st = self.c("st")
        mm = self.marks("dve")
        for tb in range(self.NCH):
            cs = slice(tb * 128, (tb + 1) * 128)
            SY.wait("st", unwrap_st)
            mm.wait(SY, 2)
            SY.dma_start(out=outstg[:, tb % 2, :], in_=self.mg_dram[tb * 128:(tb + 1) * 128, :])
            SY.inc("ld", 16)
            TE.wait("ld", self.c("ld"))
            mm.wait(TE, 2)
            for fc in range(NIC):
                TE.transpose(self.pt[tb % 2][:, fc * 128:(fc + 1) * 128],
                             outstg[:, tb % 2, fc * 128:(fc + 1) * 128], self.ident[:])
            TE.inc("pe")
            VE.wait("pe", self.c("pe"))
            VE.wait("dve", self.c("dve"))
            for fc in range(NIC):
                VE.tensor_tensor(self.x[fc][:, cs], self.x[fc][:, cs],
                                 self.pt[tb % 2][:, fc * 128:(fc + 1) * 128], ALU.add)
            VE.inc("dve")
            mm.mark()
        mb = self.marks("dve")
        for oc in range(NIC):
            for tc in range(self.NTC):
                ts = slice(tc * 512, (tc + 1) * 512)
                bank = (oc * self.NTC + tc) % 4
                if oc == 0 and tc == 0:
                    TE.wait("dve", self.c("dve"))
                    TE.wait("wld", b2_wld)
                mb.wait(TE, 4)
                TE.matmul(self.pb[bank][:, 0:512], self.b2sb[0:8, oc * 128:(oc + 1) * 128],
                          self.ew8[0:8, ts], start=True, stop=True)
                TE.inc("pe")
                VE.wait("pe", self.c("pe"))
                VE.tensor_tensor(self.x[oc][:, ts], self.x[oc][:, ts],
                                 self.pb[bank][:, 0:512], ALU.add)
                VE.inc("dve")
                mb.mark()
        self.bar()

    def ph_store(self):
        SY = self.SY
        for ic in range(NIC):
            SY.wait("dve", self.c("dve"))
            SY.dma_start(out=self.y_out[ic * 128:(ic + 1) * 128, :], in_=self.x[ic][:])
            SY.inc("st", 16)
        SY.wait("st", self.c("st"))
        self.bar(full=True)

    def emit_all(self, phases):
        ph_map = {
            "consts": self.ph_consts,
            "A": self.ph_precompute_A,
            "l0_attn": self.ph_l0_attn,
            "l0_ffn": self.ph_l0_ffn,
            "store": self.ph_store,
        }
        for ph in phases:
            ph_map[ph]()


def build_nc(T, phases, input_names):
    nc = bass.Bass("TRN2", target_bir_lowering=False)
    with contextlib.ExitStack() as stack:
        mk = MK(nc, stack, T, input_names)
        E = stack.enter_context
        mk.block = E(nc.Block())
        ph_map = {
            "consts": mk.ph_consts,
            "A": mk.ph_precompute_A,
            "l0_attn": mk.ph_l0_attn,
            "l0_ffn": mk.ph_l0_ffn,
            "l1_attn": mk.ph_l1_attn,
            "l1_ffn_attn": mk.ph_l1_ffn_attn,
            "compact": mk.ph_compact,
            "moe": mk.ph_moe,
            "store": mk.ph_store,
        }
        mid = None
        for ph in phases:
            if mid is None and ph not in ("consts", "A"):
                mid = contextlib.ExitStack()
                mk.sA = [mid.enter_context(nc.sbuf_tensor(f"sA{i}", [128, T], BF16)) for i in range(NIC)]
                mk.sB = [mid.enter_context(nc.sbuf_tensor(f"sB{i}", [128, T], BF16)) for i in range(NIC)]
                mk.zrow = mid.enter_context(nc.sbuf_tensor("zrow", [128, 512], BF16))
                mk.rw0 = mid.enter_context(nc.sbuf_tensor("rw0", [2, 1024], F32))
                mk.rw1 = mid.enter_context(nc.sbuf_tensor("rw1", [2, 1024], F32))
                mk.rw2 = mid.enter_context(nc.sbuf_tensor("rw2", [2, 1024], F32))
                mk.crow = mk.rw2[0:1, 512:1024]
                mk.rstd_bf = mid.enter_context(nc.sbuf_tensor("rstd_bf", [2, 1024], BF16))
                mk.msk0 = mid.enter_context(nc.sbuf_tensor("msk0", [128, 128], BF16))
                mk.statef = mid.enter_context(nc.sbuf_tensor("statef", [128, DH], F32))
                mk.otmp = mid.enter_context(nc.sbuf_tensor("otmp", [128, 512], F32))
                mk.tmp2 = mid.enter_context(nc.sbuf_tensor("tmp2", [128, 512], F32))
                midC = contextlib.ExitStack()
                mk.sC = [midC.enter_context(nc.sbuf_tensor(f"sC{i}", [128, T], BF16)) for i in range(3)]
            if ph == "compact" and mid:
                midC.close()
            if ph == "moe" and mid:
                mid.close()
                mk.mid0.close()
                mid = False
            mk.run_phase(ph_map[ph])
        if mid:
            mid.close()
    _mask = {}
    for _lib in _all_libs:
        for _it in _lib.instructions:
            _mask[_it] = _mask.get(_it, 0) | (1 << _lib.index)
    bass_rust.insert_library_loads(nc, _mask, len(_all_libs), _std_lib.index)
    mybir.codegen_inst_isa_subclasses(nc)
    return nc



def dft_consts():
    k = np.arange(D)
    ang = 2.0 * np.pi * np.outer(k, k) / D
    Fr = np.cos(ang).astype(np.float32)
    Fi = (-np.sin(ang)).astype(np.float32)
    Fcr = (Fr / D).astype(np.float32)
    Fci = (-Fi / D).astype(np.float32)
    return Fr, Fi, Fcr, Fci


def pack_lhsT(wT):
    K, M = wT.shape
    r = wT.reshape(K // 128, 128, M // 128, 128).transpose(2, 1, 0, 3)
    return np.ascontiguousarray(r.reshape(M // 128, 128, K).astype(np.float32))


def regroup4(pack):
    return np.ascontiguousarray(
        pack.reshape(2, 4, 128, 1024).transpose(0, 2, 1, 3).reshape(2, 128, 4096))


def pack_cols(v):
    n = v.shape[-1] // 128
    return np.ascontiguousarray(v.reshape(n, 128).T.astype(np.float32))


def host_pack(inputs, T):
    d = {}
    Fr, Fi, Fcr, Fci = dft_consts()
    def rowtiles(M):
        return np.ascontiguousarray(
            M.reshape(8, 128, 1024).transpose(1, 0, 2).reshape(128, 8192).astype(np.float32))
    d["c_FcrP"] = rowtiles(Fcr)
    d["c_FciP"] = rowtiles(Fci)
    d["c_negFciP"] = rowtiles(-Fci)
    frp = pack_lhsT(Fr)
    fip = pack_lhsT(np.ascontiguousarray(-Fi))
    d["c_FrFi"] = np.ascontiguousarray(np.concatenate([frp, fip], axis=2))
    d["c_ident"] = np.eye(128, dtype=np.float32)
    d["c_ones128"] = np.ones((128, 128), dtype=np.float32)
    d["c_onescol"] = np.ones((128, 1), dtype=np.float32)
    d["c_onesT"] = np.ones((1, max(T, 2048)), dtype=np.float32)
    zo = np.zeros((1, 2048), dtype=np.float32)
    zo[0, 512:1024] = 1.0
    zo[0, 1536:2048] = 1.0
    d["c_zo"] = zo
    d["c_maskUT"] = np.triu(np.ones((128, 128), dtype=np.float32))
    hp = np.zeros((128, 2), dtype=np.float32)
    hp[0:64, 0] = 1.0
    hp[64:128, 1] = 1.0
    d["c_hpat"] = hp
    pp = np.zeros((2, 128), dtype=np.float32)
    pp[0, :] = 1.0
    d["c_patplain"] = pp
    wq_ = pack_lhsT(inputs["L0_attn_Wq"].T)
    wk_ = pack_lhsT(inputs["L0_attn_Wk"].T)
    wv_ = pack_lhsT(inputs["L0_attn_Wv"].T)
    d["l0_wqkv"] = np.ascontiguousarray(
        np.concatenate([wv_, wq_, wk_], axis=2))
    d["l0_wo"] = regroup4(pack_lhsT(inputs["L0_attn_Wo"].T))
    wr_ = pack_lhsT(inputs["L0_ffn_w_freq_real"].T)
    wi_ = pack_lhsT(inputs["L0_ffn_w_freq_imag"].T)
    d["l0_wri"] = np.ascontiguousarray(np.concatenate([wr_, wi_], axis=2))

    def hl(g, b):
        m = np.zeros((3, 128), dtype=np.float32)
        m[0, 0:64] = g
        m[1, 64:128] = g
        m[2, 0:64] = b
        m[2, 64:128] = b
        return m
    d["g_hlq"] = hl(np.asarray(inputs["L0_attn_q_gamma"]), np.asarray(inputs["L0_attn_q_beta"]))
    d["g_hlk"] = hl(np.asarray(inputs["L0_attn_k_gamma"]), np.asarray(inputs["L0_attn_k_beta"]))

    def pat(g, b):
        return np.ascontiguousarray(np.stack([np.asarray(g), np.asarray(b)]).astype(np.float32))
    d["g_l1a"] = pat(inputs["L1_attn_ln_gamma"], inputs["L1_attn_ln_beta"])
    d["g_ln1"] = pat(inputs["L1_ffn_ln1_g"], inputs["L1_ffn_ln1_b"])
    d["g_ln2"] = pat(inputs["L1_ffn_ln2_g"], inputs["L1_ffn_ln2_b"])
    bc = np.zeros((128, 240), dtype=np.float32)
    bc[:, 0:8] = pack_cols(np.asarray(inputs["L0_ffn_bias"]))
    for l in range(4):
        bc[:, 8 + l * 8:16 + l * 8] = pack_cols(np.asarray(inputs["L1_attn_fwd_bz"])[l])
        bc[:, 40 + l * 8:48 + l * 8] = pack_cols(np.asarray(inputs["L1_attn_fwd_bh"])[l])
        bc[:, 72 + l * 8:80 + l * 8] = pack_cols(np.asarray(inputs["L1_attn_bwd_bz"])[l])
        bc[:, 104 + l * 8:112 + l * 8] = pack_cols(np.asarray(inputs["L1_attn_bwd_bh"])[l])
    bc[:, 136:144] = pack_cols(np.asarray(inputs["L1_attn_fusion_b"]))
    bc[:, 144:168] = pack_cols(np.asarray(inputs["L1_ffn_b_qkv"]))
    bc[:, 168:176] = pack_cols(np.asarray(inputs["L1_ffn_b_o"]))
    for e in range(8):
        bc[:, 176 + e * 8:184 + e * 8] = pack_cols(np.asarray(inputs["L1_ffn_b2"])[e])
    d["biascols"] = bc
    d["gate_b"] = np.asarray(inputs["L1_ffn_gate_b"]).reshape(8, 1).astype(np.float32)
    for nm, kz, kh in (("fwd_wzh", "L1_attn_fwd_Wz", "L1_attn_fwd_Wh"),
                       ("bwd_wzh", "L1_attn_bwd_Wz", "L1_attn_bwd_Wh")):
        wz = np.asarray(inputs[kz])
        wh = np.asarray(inputs[kh])
        grans = []
        for l in range(4):
            pz = pack_lhsT(wz[l].T)
            ph = pack_lhsT(wh[l].T)
            for g in range(4):
                grans.append(np.concatenate(
                    [pz[2 * g], ph[2 * g], pz[2 * g + 1], ph[2 * g + 1]], axis=1))
        d[nm] = np.ascontiguousarray(np.stack(grans))
    fw = np.asarray(inputs["L1_attn_fusion_W"]).T
    d["fus_top"] = regroup4(pack_lhsT(np.ascontiguousarray(fw[0:1024])))
    d["fus_bot"] = regroup4(pack_lhsT(np.ascontiguousarray(fw[1024:2048])))
    wqkv = pack_lhsT(np.asarray(inputs["L1_ffn_w_qkv"]))
    d["wqkv3"] = np.ascontiguousarray(np.concatenate(
        [wqkv[0:8], wqkv[8:16], wqkv[16:24]], axis=2))
    d["wo2"] = regroup4(pack_lhsT(np.asarray(inputs["L1_ffn_w_o"])))
    d["gate_pack"] = np.ascontiguousarray(
        np.asarray(inputs["L1_ffn_gate_w"]).reshape(8, 128, 8).astype(np.float32))
    w1 = np.asarray(inputs["L1_ffn_W1"]).astype(np.float32)
    w1gb = w1.reshape(8, 8, 128, 4, 1024).transpose(0, 3, 2, 1, 4).reshape(32, 128, 8192)
    w2 = np.asarray(inputs["L1_ffn_W2"]).astype(np.float32)
    w2gb = w2.reshape(8, 4, 8, 128, 1024).transpose(0, 1, 3, 2, 4).reshape(32, 128, 8192)
    b1v = np.asarray(inputs["L1_ffn_b1"]).astype(np.float32)
    wsrc = np.zeros((4096, 16512), dtype=np.float32)
    for e in range(8):
        for g in range(4):
            r0 = (e * 4 + g) * 128
            wsrc[r0:r0 + 128, 0:8192] = w1gb[e * 4 + g]
            wsrc[r0:r0 + 128, 8192:8200] = (
                b1v[e, g * 1024:(g + 1) * 1024].reshape(8, 128).T)
            wsrc[r0:r0 + 128, 8200:16392] = w2gb[e * 4 + g]
    d["wsrc"] = wsrc
    d["b2p"] = np.asarray(inputs["L1_ffn_b2"]).astype(np.float32)
    d["c_identf"] = np.eye(128, dtype=np.float32)
    d["c_irow"] = np.broadcast_to(np.arange(11, dtype=np.float32), (8, 11)).copy()
    pg = (np.arange(128, dtype=np.float32)[:, None]
          + 128.0 * np.arange(4, dtype=np.float32)[None, :])
    d["c_pg"] = np.ascontiguousarray(pg)
    eb = np.zeros((8, 2), dtype=np.float32)
    eb[:, 0] = 2048.0 * np.arange(8) - 1.0
    eb[:, 1] = 15360.0 - 1024.0 * np.arange(8)
    d["c_ebase"] = eb
    ovc = (np.arange(64, dtype=np.float32)[None, :] * 16.0
           + (np.arange(128, dtype=np.float32) % 16.0)[:, None])
    d["c_ovc"] = np.ascontiguousarray(ovc)
    return d


def input_spec(d, T):
    return [(k, v.shape) for k, v in d.items()] + [("xT", (D, T))]


def run_phases(inputs, phases, T=2048, n_cores=NCORE, x_batches=None):
    d = host_pack(inputs, T)
    spec = input_spec(d, T)
    nc = build_nc(T, phases, spec)
    x = np.asarray(inputs["L0_attn_x"], dtype=np.float32)
    in_maps = []
    bsel = x_batches if x_batches is not None else list(range(n_cores))
    for b in bsel:
        m = dict(d)
        m["xT"] = np.ascontiguousarray(x[b, :T, :].T)
        in_maps.append(m)
    res = run_bass_kernel_spmd(nc, in_maps, core_ids=list(range(len(bsel))))
    return np.stack([np.ascontiguousarray(r["y_out"].T) for r in res.results])


def kernel(**inputs):
    phases = ["consts", "A", "l0_attn", "l0_ffn", "l1_attn",
              "l1_ffn_attn", "compact", "moe", "store"]
    return run_phases(inputs, phases)
